# version 05
# baseline (speedup 1.0000x reference)
import numpy as np
import ml_dtypes

import concourse.bass as bass
import concourse.mybir as mybir
from concourse.bass_utils import run_bass_kernel_spmd

F32 = mybir.dt.float32
BF16 = mybir.dt.bfloat16
AF = mybir.ActivationFunctionType
ALU = mybir.AluOpType
AX = mybir.AxisListType.X
BFD = ml_dtypes.bfloat16

D = 1024
NT = 32
H, DH = 16, 64
MAGIC = 12582912.0
CG_ITERS = 30
ENGINES = ("T", "V", "S", "G", "Q")
CONST_VALS = [1e-5, 0.00064, 1e-6, -1.0, 1.0, 1.0 / 4194304.0, 0.0, 2.0]


class Emit:
    def __init__(self):
        self.items = {e: [] for e in ENGINES}
        self.counts = {e: 0 for e in ENGINES}
        self.hwm = {e: {} for e in ENGINES}

    def op(self, eng, fn, waits=(), inc=1):
        w = []
        hw = self.hwm[eng]
        for t in waits:
            if t is None:
                continue
            se, sv = t
            if sv <= 0 or hw.get(se, 0) >= sv:
                continue
            if se == eng and sv > self.counts[eng]:
                continue
            hw[se] = sv
            w.append((se, sv))
        self.counts[eng] += inc
        self.items[eng].append((w, fn, inc))
        return (eng, self.counts[eng])

    def dma(self, q, out, in_, waits=()):
        w = list(waits) + [(q, self.counts[q])]
        return self.op(q, lambda e: e.dma_start(out=out, in_=in_), w, inc=16)

    def barrier(self):
        toks = [(e, self.counts[e]) for e in ENGINES]
        for e in ENGINES:
            self.op(e, lambda h: h.nop(), toks, inc=0 if e == "G" else 1)
        return [(e, self.counts[e]) for e in ENGINES]

    def install(self, block, sems):
        def mk(eng):
            items = self.items[eng]
            mysem = sems[eng]
            def runner(handle):
                for w, fn, inc in items:
                    for (se, sv) in w:
                        handle.wait_ge(sems[se], sv)
                    ins = fn(handle)
                    if ins is not None and inc > 0:
                        ins.then_inc(mysem, inc)
            return runner
        block.tensor(mk("T"))
        block.vector(mk("V"))
        block.scalar(mk("S"))
        block.gpsimd(mk("G"))
        block.sync(mk("Q"))


def build_program(debug=None, phase_stop=5, l0cut=9):
    nc = bass.Bass("TRN2", target_bir_lowering=False)
    em = Emit()

    def din(name, shape, dtype=F32):
        return nc.dram_tensor(name, list(shape), dtype, kind="ExternalInput").ap()

    X0 = din("x0", [128, NT, D])
    MAAX = din("maax", [128, 8])
    MAACOL = din("maacol", [128, 4, 8])
    MAAW1 = din("maaw1", [D, 128])
    W2L = din("w2l", [64, 16, 128])
    DW1 = din("dw1", [D, DH])
    DW2L = din("dw2l", [65, 8, 128])
    UFM = din("ufm", [128, 8])
    LNW = din("lnw", [128, 8])
    LNB = din("lnb", [128, 8])
    WR = din("wr", [D, D])
    WK = din("wk", [D, D])
    WV = din("wv", [D, D])
    WO = din("wo", [D, D])
    WGT = din("wgt", [D, 4096])
    WUT = din("wut", [D, 4096])
    WDT = din("wdt", [4096, D])
    QW = din("qw", [D, D])
    KW = din("kw", [D, D])
    VW = din("vw", [D, D])
    FW = din("fw", [D, H])
    BW = din("bw", [D, H])
    OW = din("ow", [D, D])
    DELTA = din("delta", [1, H])
    LAMBF = din("lambf", [128, 8])
    NRMW = din("nrmw", [128, 8])
    GW1 = din("gw1", [D, DH])
    GW2 = din("gw2", [DH, D])
    RHS = din("rhs", [D, D])
    RHST = din("rhst", [D, D])
    IDF = din("idf", [128, 128])
    IDU8 = din("idu8", [128, 128], mybir.dt.uint8)
    CONSTI = din("consti", [128, 8])
    TRII = din("trii", [128, 128])
    TRIS = din("tris", [128, 128])
    EYEF = din("eyef", [128, 128])

    OUT = nc.dram_tensor("out", [128, NT, D], F32, kind="ExternalOutput").ap()
    DBG = None
    if debug is not None:
        DBG = nc.dram_tensor("dbg", list(debug), F32, kind="ExternalOutput").ap()

    wb = {}
    for nm, shape in [("wr", [D, D]), ("wk", [D, D]), ("wv", [D, D]), ("wo", [D, D]),
                      ("g", [D, 4096]), ("u", [D, 4096]), ("d", [4096, D]),
                      ("qw", [D, D]), ("kw", [D, D]), ("vw", [D, D]), ("ow", [D, D]),
                      ("rhs", [D, D])]:
        wb[nm] = nc.dram_tensor("wb_" + nm, shape, BF16, kind="Internal").ap()
    LAMBD = nc.dram_tensor("lamb_dram", [1, D], F32, kind="Internal").ap()
    RSRD = nc.dram_tensor("rsrow_dram", [1, D], F32, kind="Internal").ap()
    SCLD = nc.dram_tensor("scl_dram", [1, 4], F32, kind="Internal").ap()
    ROWD = nc.dram_tensor("row_dram", [1, D], F32, kind="Internal").ap()
    EFRD = nc.dram_tensor("efr_dram", [1, H], F32, kind="Internal").ap()

    ctx_global = []
    def enter(cm, store):
        t = cm.__enter__()
        store.append(cm)
        return t

    sems = {}
    for e in ENGINES:
        sems[e] = enter(nc.semaphore("sem_" + e), ctx_global)

    def sbg(name, shape, dt=F32):
        return enter(nc.sbuf_tensor(name, list(shape), dt), ctx_global)
    def psg(name, shape, dt=F32):
        return enter(nc.psum_tensor(name, list(shape), dt), ctx_global)

    X = sbg("X", [128, NT, D])
    IDT = sbg("IDT", [128, 128])
    IDB = sbg("IDB", [128, 128], BF16)
    IDU = sbg("IDU", [128, 128], mybir.dt.uint8)
    CST = sbg("CST", [128, 80])
    LAMB = sbg("LAMB", [128, 8])
    CPA = sbg("CPA", [128, 8])
    for _ci, _cv in enumerate(CONST_VALS):
        nc.const_aps.register(F32, _cv, CPA[:, _ci:_ci + 1])
    MGB = sbg("MGB", [128, 4])
    STT = sbg("STT", [128, 32])
    XN = sbg("XN", [128, D])
    SCR = sbg("SCR", [128, D])
    TMA = sbg("TMA", [128, D])
    TMB = sbg("TMB", [128, D])
    TB1 = sbg("TB1", [128, D], BF16)
    TB2 = sbg("TB2", [128, D], BF16)
    WS = sbg("WS", [128, 4, 512], BF16)
    WSG = [None, None, None, None]
    ST1 = sbg("ST1", [128, 8, DH])
    ST2 = sbg("ST2", [128, 8, DH])
    ctx_mid = []
    def sbm(name, shape, dt=F32):
        return enter(nc.sbuf_tensor(name, list(shape), dt), ctx_mid)
    TRS = sbm("TRS", [128, 128])
    BIASR = sbm("BIASR", [1, D])
    ONEC = sbm("ONEC", [1, 128])
    ST1B = sbm("ST1B", [128, 8, DH], BF16)

    PT = psg("PT", [128, 4, 128])
    PT_DRAIN = {"PT": [None]}
    PG = psg("PG", [128, 4, 512])
    PA = psg("PA", [128, 2, 512])

    qz = em.dma("Q", X[:], X0[:])
    em.dma("Q", CPA[:], CONSTI[:])
    em.dma("Q", IDU[:], IDU8[:])
    em.dma("Q", IDT[:], IDF[:])
    em.dma("Q", TRS[:], TRIS[:])
    em.dma("Q", CST[:, 0:8], MAAX[:])
    em.dma("Q", CST[:, 8:40], MAACOL.rearrange("p f j -> p (f j)"))
    em.dma("Q", CST[:, 40:48], UFM[:])
    em.dma("Q", CST[:, 48:56], LNW[:])
    em.dma("Q", CST[:, 56:64], LNB[:])
    em.dma("Q", CST[:, 64:72], LAMBF[:])
    em.dma("Q", CST[:, 72:80], NRMW[:])
    q0 = ("Q", em.counts["Q"])
    em.op("V", lambda v: v.memset(ONEC[:], 1.0), [])
    em.op("S", lambda s: s.copy(IDB[:], IDT[:]), [q0])
    tl0 = em.op("S", lambda s: s.activation(STT[:, 0:8], CST[:, 64:72], AF.Exp), [q0])
    tl = em.op("S", lambda s: s.activation(STT[:, 0:8], STT[:, 0:8], AF.Ln, bias=1.0), [tl0])
    em.op("V", lambda v: v.tensor_scalar(LAMB[:], STT[:, 0:8], 0.25, None, op0=ALU.add), [tl])

    ctx_pre = []
    def sbp(name, shape, dt=F32):
        return enter(nc.sbuf_tensor(name, list(shape), dt), ctx_pre)
    PRE = sbp("PRE", [128, 2, 1024])
    PREB = sbp("PREB", [128, 2, 1024], BF16)
    PST = sbp("PST", [128, 36])
    ROWB = sbp("ROWB", [128, 1024])
    ROW8 = sbp("ROW8", [8, 128])
    PADP = sbp("PADP", [128, 128])
    padz = em.op("V", lambda v: v.memset(PADP[:], 0.0), [])

    pre_rd = {0: None, 1: None}

    def cast_w(src, dst, row_scale=None, waits=()):
        last = None
        for j in range(8):
            b = j % 2
            w = list(waits) + [pre_rd[b]]
            ld = em.dma("Q", PRE[:, b, :], src[j * 128:(j + 1) * 128, :], waits=w)
            if row_scale is None:
                ct = em.op("S", lambda s, b=b: s.copy(PREB[:, b, :], PRE[:, b, :]), [ld])
            else:
                ct = em.op("S", lambda s, b=b, j=j: s.activation(PREB[:, b, :], PRE[:, b, :], AF.Copy, scale=row_scale(j)), [ld])
            pre_rd[b] = ct
            st = em.dma("Q", dst[j * 128:(j + 1) * 128, :], PREB[:, b, :], waits=[ct])
            last = st
        return last

    cast_w(WR, wb["wr"], waits=[q0])
    cast_w(WK, wb["wk"])
    cast_w(WV, wb["wv"])
    cast_w(WO, wb["wo"], row_scale=lambda j: CST[:, 48 + j:49 + j])
    cast_w(QW, wb["qw"])
    cast_w(KW, wb["kw"])
    cast_w(VW, wb["vw"])
    cast_w(OW, wb["ow"], row_scale=lambda j: CST[:, 72 + j:73 + j])

    for j in range(8):
        b = j % 2
        ld = em.dma("Q", PRE[:, b, :], WO[j * 128:(j + 1) * 128, :], waits=[pre_rd[b]])
        for n in range(2):
            em.op("T", lambda t, j=j, n=n, b=b: t.matmul(
                PA[0:1, 0, n * 512:(n + 1) * 512] if n == 0 else PA[0:1, 1, 0:512],
                CST[:, 56 + j:57 + j], PRE[:, b, n * 512:(n + 1) * 512],
                start=(j == 0), stop=(j == 7)), [ld, q0])
        pre_rd[b] = ("T", em.counts["T"])
    tb = ("T", em.counts["T"])
    em.op("S", lambda s: s.copy(BIASR[0:1, 0:512], PA[0:1, 0, 0:512]), [tb])
    em.op("S", lambda s: s.copy(BIASR[0:1, 512:1024], PA[0:1, 1, 0:512]), [tb])

    def quant_bl(srcT, dst, nrow, ncolslices, idx):
        sums = []
        for j in range(nrow):
            for cs in range(ncolslices):
                b = (j * ncolslices + cs) % 2
                ld = em.dma("Q", PRE[:, b, :], srcT[j * 128:(j + 1) * 128, cs * 1024:(cs + 1) * 1024], waits=[pre_rd[b]])
                rd = em.op("V", lambda v, b=b, j=j, cs=cs: v.reduce_sum(
                    PST[:, (j * ncolslices + cs):(j * ncolslices + cs) + 1],
                    PRE[:, b, :], axis=AX, apply_absolute_value=True), [ld])
                pre_rd[b] = rd
                sums.append((rd, ld))
        nsl = nrow * ncolslices
        rall = em.op("V", lambda v: v.reduce_sum(PST[:, 32:33], PST[:, 0:nsl], axis=AX), [s[0] for s in sums])
        on = em.op("V", lambda v: v.memset(PST[:, 33:34], 1.0), [])
        mm = em.op("T", lambda t: t.matmul(PG[0:1, 0, 0:1], PST[:, 32:33], PST[:, 33:34], start=True, stop=True), [rall, on, tb])
        sc = em.op("S", lambda s: s.activation(PST[0:1, 34:35], PG[0:1, 0, 0:1], AF.Copy, scale=1.0 / (nsl * 128.0 * 1024.0)), [mm])
        sc = em.op("V", lambda v: v.tensor_scalar(PST[0:1, 34:35], PST[0:1, 34:35], 1e-5, None, op0=ALU.max), [sc])
        mg1 = em.dma("Q", SCLD[0:1, idx:idx + 1], PST[0:1, 34:35], waits=[sc])
        mg = em.dma("Q", MGB[:, idx:idx + 1], SCLD[0:1, idx:idx + 1].broadcast_to([128, 1]), waits=[mg1])
        rcs = em.op("V", lambda v: v.reciprocal(PST[:, 35:36], MGB[:, idx:idx + 1]), [mg])
        lastq = None
        for j in range(nrow):
            for cs in range(ncolslices):
                b = (j * ncolslices + cs) % 2
                ld = em.dma("Q", PRE[:, b, :], srcT[j * 128:(j + 1) * 128, cs * 1024:(cs + 1) * 1024], waits=[rcs, pre_rd[b]])
                t1 = em.op("V", lambda v, b=b: v.tensor_scalar(PRE[:, b, :], PRE[:, b, :], PST[:, 35:36], MAGIC, op0=ALU.mult, op1=ALU.add), [ld])
                t2 = em.op("V", lambda v, b=b: v.tensor_scalar(PRE[:, b, :], PRE[:, b, :], MAGIC, 1.0, op0=ALU.subtract, op1=ALU.min), [t1])
                t3 = em.op("V", lambda v, b=b: v.tensor_scalar(PREB[:, b, :], PRE[:, b, :], -1.0, None, op0=ALU.max), [t2])
                pre_rd[b] = t3
                st = em.dma("Q", dst[j * 128:(j + 1) * 128, cs * 1024:(cs + 1) * 1024], PREB[:, b, :], waits=[t3])
                lastq = st
        return lastq

    quant_bl(WGT, wb["g"], 8, 4, 0)
    quant_bl(WUT, wb["u"], 8, 4, 1)
    quant_bl(WDT, wb["d"], 32, 1, 2)

    rp_state = [padz]
    def rowpack(src_ap, dst_row, deps):
        pool = PT_DRAIN["PT"]
        cp = em.op("V", lambda v: v.tensor_copy(PADP[:, 0:8], src_ap), deps + [rp_state[0]])
        tt = em.op("T", lambda t: t.transpose(PT[:, 0, :], PADP[:], IDT[:]), [cp, tb, pool[0]])
        ev = em.op("S", lambda s: s.copy(ROW8[:], PT[0:8, 0, :]), [tt])
        pool[0] = ev
        rp_state[0] = ev
        return em.dma("Q", dst_row, ROW8[:], waits=[ev])

    ams = []
    for j in range(8):
        b = j % 2
        ld = em.dma("Q", PRE[:, b, :], RHST[j * 128:(j + 1) * 128, :], waits=[pre_rd[b]])
        am = em.op("V", lambda v, b=b, j=j: v.reduce_max(PST[:, j:j + 1], PRE[:, b, :], axis=AX, apply_absolute_value=True), [ld])
        pre_rd[b] = am
        ams.append((am, ld))
    amt = em.op("V", lambda v: v.tensor_scalar(PST[:, 8:16], PST[:, 0:8], 1e-12, None, op0=ALU.max), [a[0] for a in ams])
    s1 = em.op("V", lambda v: v.tensor_scalar(PST[:, 16:24], PST[:, 8:16], 1.0 / 127.0, None, op0=ALU.mult), [amt])
    cpr = rowpack(PST[:, 16:24], RSRD[:], [s1])
    rc = em.op("V", lambda v: v.reciprocal(PST[:, 24:32], PST[:, 8:16]), [amt])
    rc = em.op("V", lambda v: v.tensor_scalar(PST[:, 24:32], PST[:, 24:32], 127.0, None, op0=ALU.mult), [rc])
    cp2 = rowpack(PST[:, 24:32], ROWD[:], [rc, cpr])
    rwb = em.dma("Q", ROWB[:], ROWD[0:1, :].broadcast_to([128, D]), waits=[cp2])
    for j in range(8):
        b = j % 2
        ld = em.dma("Q", PRE[:, b, :], RHS[j * 128:(j + 1) * 128, :], waits=[rwb, pre_rd[b]])
        t1 = em.op("V", lambda v, b=b: v.tensor_tensor(PRE[:, b, :], PRE[:, b, :], ROWB[:], op=ALU.mult), [ld])
        t2 = em.op("V", lambda v, b=b: v.tensor_scalar(PRE[:, b, :], PRE[:, b, :], MAGIC, MAGIC, op0=ALU.add, op1=ALU.subtract), [t1])
        t3 = em.op("V", lambda v, b=b: v.tensor_scalar(PRE[:, b, :], PRE[:, b, :], 127.0, -127.0, op0=ALU.min, op1=ALU.max), [t2])
        t4 = em.op("S", lambda s, b=b: s.copy(PREB[:, b, :], PRE[:, b, :]), [t3])
        pre_rd[b] = t4
        st = em.dma("Q", wb["rhs"][j * 128:(j + 1) * 128, :], PREB[:, b, :], waits=[t4])

    em.barrier()
    for cm in reversed(ctx_pre):
        cm.__exit__(None, None, None)

    def finalize_early(open_ctxs):
        em.barrier()
        sts = []
        for i in range(NT):
            sts.append(em.dma("Q", OUT[:, i, :], X[:, i, :]))
        em.op("V", lambda v: v.memset(STT[:, 31:32], 0.0), [sts[-1]])
        for ctx in open_ctxs:
            for cm in reversed(ctx):
                cm.__exit__(None, None, None)
        with nc.Block() as block:
            em.install(block, sems)
        for cm in reversed(ctx_global):
            cm.__exit__(None, None, None)
        return nc

    if phase_stop <= 1:
        return finalize_early([ctx_mid])

    def ln_tile(i, deps):
        x_ap = X[:, i, :]
        t1 = em.op("V", lambda v: v.reduce_sum(STT[:, 0:1], x_ap, axis=AX), deps)
        t2 = em.op("S", lambda s: s.activation(SCR[:], x_ap, AF.Square), deps)
        t3 = em.op("V", lambda v: v.reduce_sum(STT[:, 1:2], SCR[:], axis=AX), [t2])
        t4 = em.op("V", lambda v: v.tensor_scalar(STT[:, 2:3], STT[:, 0:1], -1.0 / D, None, op0=ALU.mult), [t1])
        t5 = em.op("V", lambda v: v.tensor_tensor(STT[:, 4:5], STT[:, 2:3], STT[:, 2:3], op=ALU.mult), [t4])
        t6 = em.op("V", lambda v: v.tensor_scalar(STT[:, 5:6], STT[:, 1:2], 1.0 / D, STT[:, 4:5], op0=ALU.mult, op1=ALU.subtract), [t3, t5])
        t7 = em.op("S", lambda s: s.activation(STT[:, 6:7], STT[:, 5:6], AF.Sqrt, bias=1e-5), [t6])
        t8 = em.op("V", lambda v: v.reciprocal(STT[:, 3:4], STT[:, 6:7]), [t7])
        t9 = em.op("V", lambda v: v.tensor_scalar(XN[:], x_ap, STT[:, 2:3], STT[:, 3:4], op0=ALU.add, op1=ALU.mult), [t8])
        return t9

    def transpose8(in_fn, out3, deps, n=8, ident=None):
        pool = PT_DRAIN["PT"]
        evs = []
        for j in range(n):
            pj = PT[:, j % 4, :]
            w = list(deps) + [pool[0]]
            tt = em.op("T", lambda t, j=j, pj=pj: t.transpose(pj, in_fn(j), IDT[:]), w)
            ev = em.op("S", lambda s, pj=pj, j=j: s.copy(out3[:, j, :], pj), [tt])
            pool[0] = ev
            evs.append(ev)
        return evs

    def transpose1(pool_name, pj, out_ap, in_ap, idn_ap, deps, ev_src=None):
        pool = PT_DRAIN["PT"]
        tt = em.op("T", lambda t: t.transpose(pj, in_ap, idn_ap), list(deps) + [pool[0]])
        src = ev_src if ev_src is not None else pj
        ev = em.op("S", lambda s: s.copy(out_ap, src), [tt])
        pool[0] = ev
        return ev


    ctx_l0 = []
    def sb0(name, shape, dt=F32):
        return enter(nc.sbuf_tensor(name, list(shape), dt), ctx_l0)
    SMW = sb0("SMW", [128, 8, 128], BF16)
    W2B = sb0("W2B", [64, 16, 128], BF16)
    DW1B = sb0("DW1B", [128, 8, DH], BF16)
    DW2B = sb0("DW2B", [65, 8, 128], BF16)
    em.dma("G", SMW[:], MAAW1.rearrange("(j p) l -> p j l", p=128))
    em.dma("G", W2B[:], W2L[:])
    em.dma("G", DW1B[:], DW1.rearrange("(j p) l -> p j l", p=128))
    em.dma("G", DW2B[:], DW2L[:])
    wl0 = ("G", em.counts["G"])
    FMA = sb0("FMA", [128, 8, 128])
    FMB = sb0("FMB", [128, 8, 128])
    FMC = sb0("FMC", [128, 8, 128])
    FME = sb0("FME", [128, 8, 128])
    MFM = sb0("MFM", [64, 2, 128], BF16)
    DL1B = sb0("DL1B", [65, 128], BF16)
    HALO = sb0("HALO", [128, 8])
    XWB = sb0("XWB", [128, 8, 128], BF16)
    XKB = sb0("XKB", [128, 8, 128], BF16)
    XVB = sb0("XVB", [128, 8, 128], BF16)
    XRB = sb0("XRB", [128, 8, 128], BF16)
    KTB = sb0("KTB", [128, 8, 128], BF16)
    RBB = sb0("RBB", [128, 8, 128], BF16)
    GHB = sb0("GHB", [128, 8, 128], BF16)
    KHT = TB2.rearrange("p (a b) -> p a b", a=8)
    RTB = XRB
    UKB = XKB
    KHB = XVB
    ZFB = XWB

    em.op("V", lambda v: v.memset(ST1.rearrange("p a b -> p (a b)"), 0.0), [])
    em.op("S", lambda s: s.copy(ST1B.rearrange("p a b -> p (a b)"), ST1.rearrange("p a b -> p (a b)")), [("V", em.counts["V"])])
    em.op("V", lambda v: v.memset(HALO[:], 0.0), [])
    em.op("V", lambda v: v.memset(DL1B.rearrange("p c -> p c"), 0.0), [])
    em.op("V", lambda v: v.memset(DL1B[64:65, :], 1.0), [("V", em.counts["V"])])
    z0 = ("V", em.counts["V"])
    s0tok = ("S", em.counts["S"])

    pg_last_read = [None, None, None, None]
    pa_last_read = [None, None]
    pt_guard = None

    resid_tok = {}

    for i in range(NT):
        lt = ln_tile(i, [qz])
        if l0cut <= 0.05:
            em.barrier()
            continue
        tev = transpose8(lambda j: XN[:, j * 128:(j + 1) * 128], FMA[:, :, 0:128], [lt])
        if l0cut <= 0.1:
            em.barrier()
            continue
        xx1 = em.op("V", lambda v: v.tensor_sub(FMB[:, :, 1:128], FMA[:, :, 0:127], FMA[:, :, 1:128]), tev)
        xx0 = em.op("V", lambda v: v.tensor_sub(FMB[:, :, 0:1], HALO[:, :, None], FMA[:, :, 0:1]), tev + [z0])
        hup = em.op("V", lambda v: v.tensor_copy(HALO[:, :, None], FMA[:, :, 127:128]), [xx1, xx0])
        z0 = hup
        if l0cut <= 0.15:
            em.barrier()
            continue
        xxt = [xx1, xx0]
        if l0cut <= 0.2:
            em.barrier()
            continue
        x3 = []
        for j in range(8):
            x3.append(em.op("V", lambda v, j=j: v.scalar_tensor_tensor(
                KTB[:, j, :], FMB[:, j, :], CST[:, j:j + 1], FMA[:, j, 0:128],
                op0=ALU.mult, op1=ALU.add), xxt + tev))
        mw = [pg_last_read[0], wl0]
        mmm = []
        for f in range(4):
            pb = 32 * (f % 2)
            for j in range(8):
                mmm.append(em.op("T", lambda t, f=f, j=j, pb=pb: t.matmul(
                    PG[pb:pb + 32, 0, (f // 2) * 128:(f // 2) * 128 + 128], SMW[:, j, f * 32:(f + 1) * 32], KTB[:, j, :],
                    start=(j == 0), stop=(j == 7)), [x3[j]] + mw))
        mtanh = em.op("S", lambda s: s.activation(MFM.rearrange("p a b -> p (a b)"), PG[0:64, 0, 0:256], AF.Tanh), [mmm[-1]])
        pg_last_read[0] = mtanh
        dsts = [XWB, XKB, XVB, XRB]
        mixtok = {0: [], 1: [], 2: [], 3: []}
        stt_prev = []
        for f in range(4):
            for j in range(8):
                bank = 1 + ((f * 8 + j) % 2)
                pj = PG[:, bank, 0:128]
                w = [mtanh, wl0] + ([stt_prev[-2]] if len(stt_prev) >= 2 else [])
                if pg_last_read[bank] is not None:
                    w.append(pg_last_read[bank])
                t1 = em.op("T", lambda t, f=f, j=j, pj=pj: t.matmul(pj, W2B[32 * (f % 2):32 * (f % 2) + 32, (f // 2) * 8 + j, :], MFM[32 * (f % 2):32 * (f % 2) + 32, f // 2, :], start=True, stop=True), w)
                t2 = em.op("V", lambda v, f=f, j=j, pj=pj: v.scalar_tensor_tensor(
                    FMC[:, j, :], pj, CST[:, 8 + f * 8 + j:9 + f * 8 + j], FMB[:, j, :], op0=ALU.add, op1=ALU.mult), [t1])
                pg_last_read[bank] = t2
                stt_prev.append(t2)
                t3 = em.op("V", lambda v, f=f, j=j, dst=dsts[f]: v.tensor_add(dst[:, j, :], FMA[:, j, 0:128], FMC[:, j, :]), [t2])
                mixtok[f].append(t3)
        if l0cut <= 0.4:
            em.barrier()
            continue
        w = [pg_last_read[3], wl0]
        for j in range(8):
            em.op("T", lambda t, j=j: t.matmul(PG[0:DH, 3, 0:128], DW1B[:, j, :], XWB[:, j, :], start=(j == 0), stop=(j == 7)), [mixtok[0][j]] + w)
        tmm = ("T", em.counts["T"])
        dtanh = em.op("S", lambda s: s.activation(DL1B[0:64, :], PG[0:DH, 3, 0:128], AF.Tanh), [tmm])
        pg_last_read[3] = dtanh
        wwm = []
        for j in range(8):
            pj = PA[:, j // 4, (j % 4) * 128:(j % 4) * 128 + 128]
            w = [dtanh, wl0] + [pa_last_read[j // 4]]
            wwm.append(em.op("T", lambda t, j=j, pj=pj: t.matmul(pj, DW2B[:, j, :], DL1B[:], start=True, stop=True), w))
        if l0cut <= 0.7:
            em.barrier()
            continue
        ew = []
        for half in range(2):
            ew.append(em.op("S", lambda s, half=half: s.activation(
                FMC[:, 4 * half:4 * half + 4, :].rearrange("p a b -> p (a b)"),
                PA[:, half, :], AF.Exp), [wwm[4 * half + 3], wwm[-1], mixtok[3][7]]))
        pa_last_read[0] = ew[0]
        pa_last_read[1] = ew[1]
        sc = []
        for j in range(8):
            sc.append(em.op("V", lambda v, j=j: v.tensor_tensor_scan(
                FME[:, j, :], FMC[:, j, :], FMC[:, j, :], 0.0, op0=ALU.add, op1=ALU.bypass), ew))
        pex = em.op("S", lambda s: s.activation(FMC.rearrange("p a b -> p (a b)"), FME.rearrange("p a b -> p (a b)"), AF.Exp), sc)
        nex = em.op("S", lambda s: s.activation(FMA[:, :, 0:128].rearrange("p a b -> p (a b)"), FME.rearrange("p a b -> p (a b)"), AF.Exp, scale=-1.0), sc)

        if l0cut <= 1:
            em.barrier()
            continue
        def tmproj(srcb, wdr, mix_t, out_tm_fp32, out_bf=None):
            mms = []
            for j in range(8):
                for half in range(2):
                    slot = (j * 2 + half) % 4
                    w = list(mix_t) + [WSG[slot]]
                    ld = em.dma("Q", WS[:, slot, :], wdr[j * 128:(j + 1) * 128, half * 512:(half + 1) * 512], waits=w)
                    wm = [ld] + [pa_last_read[half]]
                    mms.append(em.op("T", lambda t, j=j, half=half, slot=slot: t.matmul(
                        PA[:, half, :], srcb[:, j, :], WS[:, slot, :],
                        start=(j == 0), stop=(j == 7)), wm))
                    WSG[slot] = mms[-1]
            evs = []
            for half in range(2):
                if out_tm_fp32 is not None:
                    evs.append(em.op("S", lambda s, half=half: s.copy(out_tm_fp32[:, half * 512:(half + 1) * 512], PA[:, half, :]), [mms[-1]]))
                else:
                    evs.append(em.op("S", lambda s, half=half: s.copy(out_bf[:, half * 512:(half + 1) * 512], PA[:, half, :]), [mms[-1]]))
            pa_last_read[0] = evs[0]
            pa_last_read[1] = evs[1]
            return evs

        evr = tmproj(XRB, wb["wr"], mixtok[3], TMA)
        trr = transpose8(lambda j: TMA[:, j * 128:(j + 1) * 128], FMB, evr)
        evk = tmproj(XKB, wb["wk"], mixtok[1], TMB)
        trk = transpose8(lambda j: TMB[:, j * 128:(j + 1) * 128], FME, evk)
        evv = tmproj(XVB, wb["wv"], mixtok[2], None, out_bf=TB1)

        if l0cut <= 2:
            em.barrier()
            continue
        ktb = em.op("V", lambda v: v.tensor_mul(KTB.rearrange("p a b -> p (a b)"), FME.rearrange("p a b -> p (a b)"), FMC.rearrange("p a b -> p (a b)")), trk + [pex, mmm[-1]])
        rtb1 = em.op("V", lambda v: v.tensor_mul(RTB[:, :, 1:128], FMB[:, :, 1:128], FMA[:, :, 0:127]), trr + [nex])
        rtb0 = em.op("V", lambda v: v.tensor_copy(RTB[:, :, 0:1], FMB[:, :, 0:1]), trr)
        ukb = em.op("V", lambda v: v.tensor_tensor(UKB[:], FME[:], CST[:, 40:48, None].broadcast_to([128, 8, 128]), op=ALU.mult), trk)
        rbb = em.op("S", lambda s: s.copy(RBB.rearrange("p a b -> p (a b)"), FMB.rearrange("p a b -> p (a b)")), trr + [rtb1, rtb0])
        kh1 = em.op("V", lambda v: v.tensor_tensor(FMC[:], FMC[:], FMA[:, :, 127:128].broadcast_to([128, 8, 128]), op=ALU.mult), [pex, nex, ktb])
        kh2 = em.op("V", lambda v: v.tensor_mul(FMB.rearrange("p a b -> p (a b)"), FME.rearrange("p a b -> p (a b)"), FMC.rearrange("p a b -> p (a b)")), [kh1, evv[-1], rbb, rtb1, rtb0])
        trkh = transpose8(lambda j: FMB[:, j, :], KHT, [kh2])

        if l0cut <= 3:
            em.barrier()
            continue
        ytoks = []
        ymm_all = []
        prev_half_tok = [None]
        for half in range(2):
            g1m = []
            g2m = []
            gprev = [None]
            for hh in range(8):
                h = half * 8 + hh
                j = h // 2
                rb = (h % 2) * 64
                p1 = PG[:, hh // 4, (hh % 4) * 128:(hh % 4) * 128 + 128]
                p2 = PG[:, 2 + hh // 4, (hh % 4) * 128:(hh % 4) * 128 + 128]
                w1 = [ktb, rtb1, rtb0, pg_last_read[hh // 4], trkh[-1], gprev[0]]
                if l0cut == 3.03:
                    g1m.append(em.op("T", lambda t, p1=p1: t.matmul(
                        p1, KTB[0:64, 0, :], KTB[0:64, 0, :], start=True, stop=True), w1))
                elif l0cut == 3.01:
                    g1m.append(em.op("T", lambda t: t.matmul(
                        PG[:, 0, 0:128], KTB[0:64, 0, :], KTB[0:64, 0, :], start=True, stop=True), w1))
                elif l0cut == 3.04:
                    g1m.append(em.op("T", lambda t, hh=hh: t.matmul(
                        PG[:, 0, 0:128], KTB[0:64, 0, :], KTB[0:64, 0, :], start=(hh % 8 == 0), stop=(hh % 8 == 7)), w1))
                elif l0cut == 3.06:
                    g1m.append(em.op("T", lambda t, p1=p1: t.matmul(
                        p1, KTB[0:64, 0, :], KTB[0:64, 0, :], start=True, stop=True), w1 + [g1m[-1] if g1m else None]))
                elif l0cut == 3.02:
                    pa_t = PA[:, hh // 4, (hh % 4) * 128:(hh % 4) * 128 + 128]
                    g1m.append(em.op("T", lambda t, pa_t=pa_t: t.matmul(
                        pa_t, KTB[0:64, 0, :], KTB[0:64, 0, :], start=True, stop=True), w1))
                elif l0cut != 3.08:
                    g1m.append(em.op("T", lambda t, j=j, rb=rb, p1=p1: t.matmul(
                        p1, KTB[rb:rb + 64, j, :], RTB[rb:rb + 64, j, :], start=True, stop=True), w1))
                    gprev[0] = g1m[-1]
                w2 = [ukb, rbb, pg_last_read[2 + hh // 4], gprev[0]]
                if l0cut != 3.05:
                    g2m.append(em.op("T", lambda t, j=j, rb=rb, p2=p2: t.matmul(
                        p2, UKB[rb:rb + 64, j, :], RBB[rb:rb + 64, j, :], start=True, stop=True), w2))
                    gprev[0] = g2m[-1]
            if l0cut <= 3.2:
                continue
            ma = em.op("V", lambda v: v.tensor_tensor(
                SCR.rearrange("p (a b) -> p a b", a=8),
                PG[:, 0:2, :].rearrange("p a b -> p (a b)").rearrange("p (a b) -> p a b", a=8),
                TRS[:, None, :].broadcast_to([128, 8, 128]), op=ALU.mult),
                g1m + [lt, prev_half_tok[0]])
            mb = em.op("V", lambda v: v.tensor_tensor(
                TMA.rearrange("p (a b) -> p a b", a=8),
                PG[:, 2:4, :].rearrange("p a b -> p (a b)").rearrange("p (a b) -> p a b", a=8),
                IDT[:, None, :].broadcast_to([128, 8, 128]), op=ALU.mult),
                g2m + [trr[-1], prev_half_tok[0]])
            m2 = em.op("V", lambda v: v.tensor_add(
                GHB.rearrange("p a b -> p (a b)"), SCR[:], TMA[:]),
                [ma, mb] + ([ymm_all[-1]] if ymm_all else []))
            prev_half_tok[0] = m2
            for b in range(4):
                pg_last_read[b] = m2
            if l0cut <= 3.5:
                continue
            for hh in range(8):
                h = half * 8 + hh
                j = h // 2
                rb = (h % 2) * 64
                pa_ap = PA[:, h // 8, (h % 8) * 64:(h % 8) * 64 + 64]
                w = [m2, evv[-1], pa_last_read[h // 8] if hh == 0 else None]
                ymm_all.append(em.op("T", lambda t, hh=hh, h=h, pa_ap=pa_ap: t.matmul(
                    pa_ap, GHB[:, hh, :], TB1[:, h * 64:h * 64 + 64], start=True, stop=False), w))
                ymm_all.append(em.op("T", lambda t, j=j, rb=rb, h=h, pa_ap=pa_ap: t.matmul(
                    pa_ap, RTB[rb:rb + 64, j, :], ST1B[rb:rb + 64, j, :], start=False, stop=True), [s0tok]))
        ylast = ymm_all[-1] if ymm_all else lt

        if l0cut <= 4:
            em.barrier()
            continue
        stm = []
        for h in range(16):
            j = h // 2
            rb = (h % 2) * 64
            w = [trkh[-1], evv[-1], pg_last_read[0] if h == 0 else None]
            stm.append(em.op("T", lambda t, j=j, rb=rb, h=h: t.matmul(
                PG[rb:rb + 64, 0, (h // 2) * 64:(h // 2) * 64 + 64],
                KHT[:, j, rb:rb + 64], TB1[:, h * 64:h * 64 + 64], start=True, stop=True), w))
        sup = []
        for pr in range(8):
            sup.append(em.op("V", lambda v, pr=pr: v.scalar_tensor_tensor(
                ST1[:, pr, :], ST1[:, pr, :], FMA[:, pr, 127:128], PG[:, 0, pr * 64:pr * 64 + 64],
                op0=ALU.mult, op1=ALU.add), [stm[-1], s0tok, ylast]))
        s0tok = em.op("S", lambda s: s.copy(ST1B.rearrange("p a b -> p (a b)"), ST1.rearrange("p a b -> p (a b)")), sup)
        pg_last_read[0] = s0tok

        if l0cut <= 5:
            em.barrier()
            continue
        gm = em.op("V", lambda v: v.tensor_reduce(STT[:, 16:32], PA.rearrange("p a b -> p (a b)").rearrange("p (h i) -> p h i", h=16), axis=AX, op=ALU.add), [ylast])
        gsq = em.op("S", lambda s: s.activation(SCR[:], PA.rearrange("p a b -> p (a b)"), AF.Square), [ylast, prev_half_tok[0]])
        gs = em.op("V", lambda v: v.tensor_reduce(STT[:, 0:16], SCR.rearrange("p (h i) -> p h i", h=16), axis=AX, op=ALU.add), [gsq])
        gmm = em.op("V", lambda v: v.tensor_scalar(STT[:, 16:32], STT[:, 16:32], 1.0 / DH, None, op0=ALU.mult), [gm])
        gv1 = em.op("V", lambda v: v.tensor_tensor(SCR[:, 0:16], STT[:, 16:32], STT[:, 16:32], op=ALU.mult), [gmm, gs])
        gv2 = em.op("V", lambda v: v.tensor_scalar(SCR[:, 16:32], STT[:, 0:16], 1.0 / DH, None, op0=ALU.mult), [gs])
        gv3 = em.op("V", lambda v: v.tensor_sub(SCR[:, 0:16], SCR[:, 16:32], SCR[:, 0:16]), [gv1, gv2])
        gsd = em.op("S", lambda s: s.activation(SCR[:, 16:32], SCR[:, 0:16], AF.Sqrt, bias=0.00064), [gv3])
        grs = em.op("V", lambda v: v.reciprocal(SCR[:, 0:16], SCR[:, 16:32]), [gsd])
        z1 = em.op("V", lambda v: v.tensor_tensor(TMA.rearrange("p (h i) -> p h i", h=16),
            PA.rearrange("p a b -> p (a b)").rearrange("p (h i) -> p h i", h=16),
            STT[:, 16:32, None].broadcast_to([128, 16, 64]), op=ALU.subtract), [gmm, ylast])
        z2 = em.op("V", lambda v: v.tensor_tensor(TMB.rearrange("p (h i) -> p h i", h=16),
            TMA.rearrange("p (h i) -> p h i", h=16),
            SCR[:, 0:16, None].broadcast_to([128, 16, 64]), op=ALU.mult), [z1, grs, stm[-1]])
        pa_last_read[0] = z2
        pa_last_read[1] = z2
        trz = transpose8(lambda j: TMB[:, j * 128:(j + 1) * 128], ZFB, [z2])
        if l0cut <= 6:
            em.barrier()
            continue
        oml = []
        for j in range(8):
            for half in range(2):
                slot = (j * 2 + half) % 4
                w = [trz[j], WSG[slot]]
                ld = em.dma("Q", WS[:, slot, :], wb["wo"][j * 128:(j + 1) * 128, half * 512:(half + 1) * 512], waits=w)
                wm = [ld] + [pa_last_read[half] if j == 0 else None]
                oml.append(em.op("T", lambda t, j=j, half=half, slot=slot: t.matmul(
                    PA[:, half, :], ZFB[:, j, :], WS[:, slot, :], start=(j == 0), stop=False), wm))
                WSG[slot] = oml[-1]
        for half in range(2):
            oml.append(em.op("T", lambda t, half=half: t.matmul(
                PA[:, half, :], ONEC[:], BIASR[0:1, half * 512:(half + 1) * 512], start=False, stop=True), [q0]))
        rt = []
        for half in range(2):
            rt.append(em.op("V", lambda v, half=half, i=i: v.tensor_add(
                X[:, i, half * 512:(half + 1) * 512], X[:, i, half * 512:(half + 1) * 512], PA[:, half, :]), [oml[-1]]))
        resid_tok[i] = rt[-1]
        pa_last_read[0] = rt[0]
        pa_last_read[1] = rt[1]
        em.barrier()

    em.barrier()
    for cm in reversed(ctx_l0):
        cm.__exit__(None, None, None)
    for cm in reversed(ctx_mid):
        cm.__exit__(None, None, None)

    if phase_stop <= 2:
        return finalize_early([])

    ctx_f = []
    def sbf(name, shape, dt=F32):
        return enter(nc.sbuf_tensor(name, list(shape), dt), ctx_f)
    XQF = sbf("XQF", [128, 2, 8, 128], BF16)
    HB = sbf("HB", [128, 2, 4096])
    FB = sbf("FB", [128, 4, 128], BF16)
    GSC = sbf("GSC", [128, 12])

    pa_g = [None, None]
    pg_g = [None, None, None, None]
    pt_g = None
    ffn_res = {}
    for sblk in range(NT // 2):
        tiles = [2 * sblk, 2 * sblk + 1]
        amax_toks = []
        gsc_toks = []
        for tt_, i in enumerate(tiles):
            lt = ln_tile(i, [resid_tok[i]])
            am = em.op("V", lambda v, tt_=tt_: v.reduce_max(STT[:, 8 + tt_:9 + tt_], XN[:], axis=AX, apply_absolute_value=True), [lt])
            am = em.op("V", lambda v, tt_=tt_: v.tensor_scalar(STT[:, 8 + tt_:9 + tt_], STT[:, 8 + tt_:9 + tt_], 1e-5, None, op0=ALU.max), [am])
            rc = em.op("V", lambda v, tt_=tt_: v.reciprocal(STT[:, 10 + tt_:11 + tt_], STT[:, 8 + tt_:9 + tt_]), [am])
            rc = em.op("V", lambda v, tt_=tt_: v.tensor_scalar(STT[:, 10 + tt_:11 + tt_], STT[:, 10 + tt_:11 + tt_], 127.0, None, op0=ALU.mult), [rc])
            q1 = em.op("V", lambda v, tt_=tt_: v.tensor_scalar(SCR[:], XN[:], STT[:, 10 + tt_:11 + tt_], MAGIC, op0=ALU.mult, op1=ALU.add), [rc])
            q2 = em.op("V", lambda v: v.tensor_scalar(SCR[:], SCR[:], MAGIC, 127.0, op0=ALU.subtract, op1=ALU.min), [q1])
            q3 = em.op("V", lambda v: v.tensor_scalar(TMB[:], SCR[:], -128.0, None, op0=ALU.max), [q2, amax_toks[-1][-1] if amax_toks else None])
            trq = transpose8(lambda j: TMB[:, j * 128:(j + 1) * 128], XQF[:, tt_], [q3])
            gsw1 = em.op("V", lambda v, tt_=tt_: v.tensor_scalar(GSC[:, tt_ * 6:tt_ * 6 + 1], STT[:, 8 + tt_:9 + tt_], MGB[:, 0:1], 1.0 / 127.0, op0=ALU.mult, op1=ALU.mult), [am, q0])
            gsw2 = em.op("V", lambda v, tt_=tt_: v.tensor_scalar(GSC[:, tt_ * 6 + 1:tt_ * 6 + 2], STT[:, 8 + tt_:9 + tt_], MGB[:, 1:2], 1.0 / 127.0, op0=ALU.mult, op1=ALU.mult), [am])
            gsw3 = em.op("V", lambda v, tt_=tt_: v.memset(GSC[:, tt_ * 6 + 2:tt_ * 6 + 3], 0.0), [])
            gsc_toks.append(gsw3)
            amax_toks.append(trq)
        hq_last = []
        tm_guard = {0: None, 1: None}
        for ns in range(8):
            mmg = []
            mmu = []
            for kj in range(8):
                sg_ = kj % 2
                su_ = 2 + kj % 2
                w = [amax_toks[0][-1], amax_toks[1][-1]]
                ldg = em.dma("Q", WS[:, sg_, :], wb["g"][kj * 128:(kj + 1) * 128, ns * 512:(ns + 1) * 512], waits=w + [WSG[sg_]])
                ldu = em.dma("Q", WS[:, su_, :], wb["u"][kj * 128:(kj + 1) * 128, ns * 512:(ns + 1) * 512], waits=w + [WSG[su_]])
                for tt_ in range(2):
                    mmg.append(em.op("T", lambda t, kj=kj, tt_=tt_, sg_=sg_: t.matmul(
                        PA[:, tt_, :], XQF[:, tt_, kj, :], WS[:, sg_, :], start=(kj == 0), stop=(kj == 7)),
                        [ldg, pa_g[tt_] if kj == 0 else None]))
                    mmu.append(em.op("T", lambda t, kj=kj, tt_=tt_, su_=su_: t.matmul(
                        PG[:, tt_, :], XQF[:, tt_, kj, :], WS[:, su_, :], start=(kj == 0), stop=(kj == 7)),
                        [ldu, pg_g[tt_] if kj == 0 else None]))
                WSG[sg_] = mmg[-1]
                WSG[su_] = mmu[-1]
            hb_prev = None
            for tt_ in range(2):
                off = tt_ * 512
                sg1 = em.op("S", lambda s, tt_=tt_, off=off: s.activation(TMA[:, off:off + 512], PA[:, tt_, :], AF.Silu, scale=GSC[:, tt_ * 6:tt_ * 6 + 1]), [mmg[-1], gsc_toks[-1], tm_guard[tt_]])
                uv1 = em.op("V", lambda v, tt_=tt_, off=off: v.tensor_scalar(TMB[:, off:off + 512], PG[:, tt_, :], GSC[:, tt_ * 6 + 1:tt_ * 6 + 2], None, op0=ALU.mult), [mmu[-1], gsc_toks[-1], tm_guard[tt_]])
                hb1 = em.op("V", lambda v, tt_=tt_, ns=ns, off=off: v.tensor_mul(HB[:, tt_, ns * 512:(ns + 1) * 512], TMA[:, off:off + 512], TMB[:, off:off + 512]), [sg1, uv1])
                tm_guard[tt_] = hb1
                pa_g[tt_] = sg1
                pg_g[tt_] = uv1
                hm = em.op("V", lambda v, tt_=tt_, ns=ns: v.reduce_max(STT[:, 14 + tt_:15 + tt_], HB[:, tt_, ns * 512:(ns + 1) * 512], axis=AX, apply_absolute_value=True), [hb1])
                hq_last.append(em.op("V", lambda v, tt_=tt_: v.tensor_max(GSC[:, tt_ * 6 + 2:tt_ * 6 + 3], GSC[:, tt_ * 6 + 2:tt_ * 6 + 3], STT[:, 14 + tt_:15 + tt_]), [hm]))
        qdone = []
        for tt_ in range(2):
            am = em.op("V", lambda v, tt_=tt_: v.tensor_scalar(GSC[:, tt_ * 6 + 2:tt_ * 6 + 3], GSC[:, tt_ * 6 + 2:tt_ * 6 + 3], 1e-5, None, op0=ALU.max), hq_last)
            rc = em.op("V", lambda v, tt_=tt_: v.reciprocal(GSC[:, tt_ * 6 + 3:tt_ * 6 + 4], GSC[:, tt_ * 6 + 2:tt_ * 6 + 3]), [am])
            rc = em.op("V", lambda v, tt_=tt_: v.tensor_scalar(GSC[:, tt_ * 6 + 3:tt_ * 6 + 4], GSC[:, tt_ * 6 + 3:tt_ * 6 + 4], 127.0, None, op0=ALU.mult), [rc])
            em.op("V", lambda v, tt_=tt_: v.tensor_scalar(GSC[:, tt_ * 6 + 4:tt_ * 6 + 5], GSC[:, tt_ * 6 + 2:tt_ * 6 + 3], MGB[:, 2:3], 1.0 / 127.0, op0=ALU.mult, op1=ALU.mult), [am])
            for qs in range(4):
                q1 = em.op("V", lambda v, tt_=tt_, qs=qs: v.tensor_scalar(SCR[:], HB[:, tt_, qs * 1024:(qs + 1) * 1024], GSC[:, tt_ * 6 + 3:tt_ * 6 + 4], MAGIC, op0=ALU.mult, op1=ALU.add), [rc, qdone[-1] if qdone else None])
                q2 = em.op("V", lambda v: v.tensor_scalar(SCR[:], SCR[:], MAGIC, 127.0, op0=ALU.subtract, op1=ALU.min), [q1])
                qdone.append(em.op("V", lambda v, tt_=tt_, qs=qs: v.tensor_scalar(HB[:, tt_, qs * 1024:(qs + 1) * 1024], SCR[:], -128.0, None, op0=ALU.max), [q2]))
        fb_guard = [None] * 4
        dmm_by = {(tt_, half): [] for tt_ in range(2) for half in range(2)}
        for kj in range(32):
            par = kj % 2
            for half in range(2):
                wslot = half * 2 + par
                ld = em.dma("Q", WS[:, wslot, :], wb["d"][kj * 128:(kj + 1) * 128, half * 512:(half + 1) * 512],
                            waits=[WSG[wslot]])
                for tt_ in range(2):
                    fslot = tt_ * 2 + par
                    if half == 0:
                        tw = list(qdone) + [fb_guard[fslot], PT_DRAIN["PT"][0]]
                        tt1 = em.op("T", lambda t, kj=kj, tt_=tt_, fslot=fslot: t.transpose(
                            PT[:, fslot, :], HB[:, tt_, kj * 128:(kj + 1) * 128], IDT[:]), tw)
                        ev = em.op("S", lambda s, fslot=fslot: s.copy(FB[:, fslot, :], PT[:, fslot, :]), [tt1])
                        fb_guard[fslot] = ev
                        PT_DRAIN["PT"][0] = ev
                    pv = PA[:, half, :] if tt_ == 0 else PG[:, 2 + half, :]
                    gw = [fb_guard[fslot], ld]
                    if kj == 0:
                        gw.append(pa_g[half] if tt_ == 0 else pg_g[2 + half] if len(pg_g) > 2 + half else None)
                    mmtok = em.op("T", lambda t, pv=pv, fslot=fslot, wslot=wslot, kj=kj: t.matmul(
                        pv, FB[:, fslot, :], WS[:, wslot, :], start=(kj == 0), stop=(kj == 31)), gw)
                    dmm_by[(tt_, half)].append(mmtok)
                    fb_guard[fslot] = mmtok
                WSG[wslot] = dmm_by[(1, half)][-1]
        last_d = [dmm_by[(tt_, half)][-1] for tt_ in range(2) for half in range(2)]
        for tt_, i in enumerate(tiles):
            for half in range(2):
                pv = PA[:, half, :] if tt_ == 0 else PG[:, 2 + half, :]
                rt = em.op("V", lambda v, pv=pv, i=i, half=half, tt_=tt_: v.scalar_tensor_tensor(
                    X[:, i, half * 512:(half + 1) * 512], pv, GSC[:, tt_ * 6 + 4:tt_ * 6 + 5],
                    X[:, i, half * 512:(half + 1) * 512], op0=ALU.mult, op1=ALU.add), last_d)
                ffn_res[i] = rt
                if tt_ == 0:
                    pa_g[half] = rt
                else:
                    pg_g[2 + half] = rt
            pg_g[tt_] = ffn_res[i]
        em.barrier()

    em.barrier()
    for cm in reversed(ctx_f):
        cm.__exit__(None, None, None)

    if phase_stop <= 3:
        return finalize_early([])

    ctx_1 = []
    def sb1(name, shape, dt=F32):
        return enter(nc.sbuf_tensor(name, list(shape), dt), ctx_1)
    FWB = sb1("FWB", [128, 8, H], BF16)
    BWB = sb1("BWB", [128, 8, H], BF16)
    GW1B = sb1("GW1B", [128, 8, DH], BF16)
    GW2B = sb1("GW2B", [DH, D], BF16)
    DELTB = sb1("DELTB", [128, H])
    TRI = sb1("TRI", [128, 128])
    em.dma("Q", TRI[:], TRII[:])
    em.dma("G", FWB[:], FW.rearrange("(j p) h -> p j h", p=128))
    em.dma("G", BWB[:], BW.rearrange("(j p) h -> p j h", p=128))
    em.dma("G", GW1B[:], GW1.rearrange("(j p) l -> p j l", p=128))
    em.dma("G", GW2B[:], GW2[:])
    dlb = em.dma("Q", DELTB[:], DELTA[0:1, :].broadcast_to([128, H]))
    wl1 = ("Q", em.counts["Q"])
    wl1g = ("G", em.counts["G"])
    XNF = sb1("XNF", [128, 8, 128], BF16)
    K2B = sb1("K2B", [128, 8, 128], BF16)
    KTMB = sb1("KTMB", [128, D], BF16)
    K3TB = sb1("K3TB", [128, D], BF16)
    VTB = sb1("VTB", [128, D], BF16)
    PHB = sb1("PHB", [128, 8, 128], BF16)
    GHL = sb1("GHL", [128, 16, 128], BF16)
    HBD = sb1("HBD", [128, 8, 128], BF16)
    HVD = sb1("HVD", [128, 8, 128], BF16)
    CGP = sb1("CGP", [128, D])
    CGR = sb1("CGR", [128, D])
    CGX = sb1("CGX", [128, D])
    GTF = sb1("GTF", [128, D])
    EFT = sb1("EFT", [128, 16])
    RSC = sb1("RSC", [128, 80])
    SML = sb1("SML", [16, 160])
    SML2 = sb1("SML2", [16, 128])
    G1B = sb1("G1B", [64, 128], BF16)
    EFEF = sb1("EFEF", [128, 16])

    em.op("V", lambda v: v.memset(ST1.rearrange("p a b -> p (a b)"), 0.0), [])
    em.op("V", lambda v: v.memset(ST2.rearrange("p a b -> p (a b)"), 0.0), [])
    em.op("V", lambda v: v.memset(HBD.rearrange("p a b -> p (a b)"), 0.0), [])
    em.op("V", lambda v: v.memset(HVD.rearrange("p a b -> p (a b)"), 0.0), [])
    LAMBB = sb1("LAMBB", [128, D])
    LR8 = sb1("LR8", [8, 128])
    PAD1 = sb1("PAD1", [128, 128])
    PAD2 = sb1("PAD2", [128, 128])
    pad2z = em.op("V", lambda v: v.memset(PAD2[:], 0.0), [])
    g1c_guard = [None]
    _pz = em.op("V", lambda v: v.memset(PAD1[:], 0.0), [])
    _pc = em.op("V", lambda v: v.tensor_copy(PAD1[:, 0:8], LAMB[:]), [_pz])
    _e = transpose1("PT", PT[:, 0, :], LR8[:], PAD1[:], IDT[:], [_pc], ev_src=PT[0:8, 0, :])
    _d = em.dma("Q", LAMBD[:], LR8[:], waits=[_e])
    lrt = em.dma("Q", LAMBB[:], LAMBD[0:1, :].broadcast_to([128, D]), waits=[_d])

    pg_g = [None, None, None, None]
    pa_g = [None, None]
    stb_tok = ("S", em.counts["S"])
    l1_res = {}

    for i in range(NT):
        lt = ln_tile(i, [ffn_res[i]])
        trx = transpose8(lambda j: XN[:, j * 128:(j + 1) * 128], XNF, [lt])
        def tmproj2(wdr, out_fp32, out_bf, scale_guard=None):
            mms = []
            for j in range(8):
                for half in range(2):
                    slot = (j * 2 + half) % 4
                    w = [trx[j], WSG[slot]]
                    ld = em.dma("Q", WS[:, slot, :], wdr[j * 128:(j + 1) * 128, half * 512:(half + 1) * 512], waits=w)
                    mms.append(em.op("T", lambda t, j=j, half=half, slot=slot: t.matmul(
                        PA[:, half, :], XNF[:, j, :], WS[:, slot, :],
                        start=(j == 0), stop=(j == 7)), [ld, pa_g[half] if j == 0 else None]))
                    WSG[slot] = mms[-1]
            evs = []
            for half in range(2):
                if out_fp32 is not None:
                    evs.append(em.op("S", lambda s, half=half: s.copy(out_fp32[:, half * 512:(half + 1) * 512], PA[:, half, :]), [mms[-1]]))
                else:
                    evs.append(em.op("S", lambda s, half=half: s.copy(out_bf[:, half * 512:(half + 1) * 512], PA[:, half, :]), [mms[-1]]))
            pa_g[0] = evs[0]
            pa_g[1] = evs[1]
            return evs
        evq = tmproj2(wb["qw"], TMA, None)
        evk = tmproj2(wb["kw"], TMB, None)
        evv = tmproj2(wb["vw"], None, VTB)
        smm = []
        for j in range(8):
            w0 = [trx[j], wl1g] + ([pg_g[3]] if j == 0 else [])
            smm.append(em.op("T", lambda t, j=j: t.matmul(PG[:, 3, 0:16], XNF[:, j, :], FWB[:, j, :], start=(j == 0), stop=(j == 7)), w0))
        for j in range(8):
            smm.append(em.op("T", lambda t, j=j: t.matmul(PG[:, 3, 16:32], XNF[:, j, :], BWB[:, j, :], start=(j == 0), stop=(j == 7)), [trx[j]]))
        for j in range(8):
            smm.append(em.op("T", lambda t, j=j: t.matmul(PG[:, 3, 32:96], XNF[:, j, :], GW1B[:, j, :], start=(j == 0), stop=(j == 7)), [trx[j]]))
        sml = smm[-1]
        zfd = em.op("V", lambda v: v.tensor_add(RSC[:, 16:32], PG[:, 3, 0:16], DELTB[:]), [sml, dlb])
        sp0 = em.op("S", lambda s: s.activation(RSC[:, 32:48], RSC[:, 16:32], AF.Exp, scale=-1.0), [zfd])
        spl = em.op("S", lambda s: s.activation(PAD1[:, 0:16], RSC[:, 32:48], AF.Ln, bias=1.0), [sp0, _e])
        bet = em.op("S", lambda s: s.activation(RSC[:, 48:64], PG[:, 3, 16:32], AF.Sigmoid), [sml])
        g1c = em.op("S", lambda s: s.copy(PAD2[:, 0:64], PG[:, 3, 32:96]), [sml, g1c_guard[0]])
        pg_g[3] = g1c
        g1f = transpose1("PT", PT[:, 1, :], G1B[:], PAD2[:], IDT[:], [g1c, pad2z], ev_src=PT[0:64, 1, :])
        g1c_guard[0] = g1f
        gmm = []
        for half in range(2):
            gmm.append(em.op("T", lambda t, half=half: t.matmul(
                PA[:, half, :], G1B[:], GW2B[:, half * 512:(half + 1) * 512], start=True, stop=True),
                [g1f, wl1g, pa_g[half]]))
        gsg = []
        for half in range(2):
            gsg.append(em.op("S", lambda s, half=half: s.activation(GTF[:, half * 512:(half + 1) * 512], PA[:, half, :], AF.Sigmoid), [gmm[half]]))
        pa_g[0] = gsg[0]
        pa_g[1] = gsg[1]
        ev1 = transpose1("PT", PT[:, 1, :], SML[:, 0:128], PAD1[:], IDT[:], [spl], ev_src=PT[0:16, 1, :])
        scn = em.op("V", lambda v: v.tensor_tensor_scan(SML[:, 0:128], SML[:, 0:128], SML[:, 0:128], 0.0, op0=ALU.add, op1=ALU.bypass), [ev1])
        ev2 = transpose1("PT", PT[:, 2, 0:16], RSC[:, 64:80], SML[:, 0:128], IDT[0:16, 0:16], [scn])
        eft = em.op("S", lambda s: s.activation(EFT[:], RSC[:, 64:80], AF.Exp, scale=-1.0), [ev2])
        enf = em.op("S", lambda s: s.activation(RSC[:, 16:32], RSC[:, 64:80], AF.Exp), [ev2, sp0])
        cft = em.op("V", lambda v: v.tensor_mul(RSC[:, 16:32], RSC[:, 16:32], RSC[:, 48:64]), [enf, bet])
        ksq = em.op("S", lambda s: s.activation(SCR[:], TMB[:], AF.Square), evk)
        kss = em.op("V", lambda v: v.tensor_reduce(RSC[:, 0:16], SCR.rearrange("p (h i) -> p h i", h=H), axis=AX, op=ALU.add), [ksq])
        ksd = em.op("S", lambda s: s.activation(RSC[:, 0:16], RSC[:, 0:16], AF.Sqrt, bias=1e-6), [kss])
        krc = em.op("V", lambda v: v.reciprocal(RSC[:, 0:16], RSC[:, 0:16]), [ksd])
        knm = em.op("V", lambda v: v.tensor_tensor(TMB.rearrange("p (h i) -> p h i", h=H), TMB.rearrange("p (h i) -> p h i", h=H), RSC[:, 0:16, None].broadcast_to([128, H, DH]), op=ALU.mult), [krc])
        qsq = em.op("S", lambda s: s.activation(SCR[:], TMA[:], AF.Square), evq + [knm])
        qss = em.op("V", lambda v: v.tensor_reduce(RSC[:, 0:16], SCR.rearrange("p (h i) -> p h i", h=H), axis=AX, op=ALU.add), [qsq])
        qsd = em.op("S", lambda s: s.activation(RSC[:, 0:16], RSC[:, 0:16], AF.Sqrt, bias=1e-6), [qss])
        qrc = em.op("V", lambda v: v.reciprocal(RSC[:, 0:16], RSC[:, 0:16]), [qsd])
        qnm = em.op("V", lambda v: v.tensor_tensor(CGP.rearrange("p (h i) -> p h i", h=H), TMA.rearrange("p (h i) -> p h i", h=H), RSC[:, 0:16, None].broadcast_to([128, H, DH]), op=ALU.mult), [qrc])
        nend = em.op("V", lambda v: v.tensor_scalar(SML[:, 128:129], SML[:, 127:128], -1.0, None, op0=ALU.mult), [scn])
        e3r = em.op("S", lambda s: s.activation(SML2[:, 0:128], SML[:, 0:128], AF.Exp, scale=1.0, bias=SML[:, 128:129]), [nend])
        pool_pt = PT_DRAIN["PT"]
        tre = em.op("T", lambda t: t.transpose(PT[:, 0, 0:16], SML2[:, 0:128], IDT[0:16, 0:16]), [e3r, pool_pt[0]])
        ev3 = em.op("S", lambda s: s.copy(SCR[:, 0:16], PT[0:128, 0, 0:16]), [tre, qss])
        pool_pt[0] = ev3
        c3t = em.op("V", lambda v: v.tensor_mul(SCR[:, 0:16], SCR[:, 0:16], RSC[:, 48:64]), [ev3, bet])
        k3b = em.op("V", lambda v: v.tensor_tensor(K3TB.rearrange("p (h i) -> p h i", h=H), TMB.rearrange("p (h i) -> p h i", h=H), SCR[:, 0:16, None].broadcast_to([128, H, DH]), op=ALU.mult), [c3t, knm])
        ktc = em.op("S", lambda s: s.copy(KTMB[:], TMB[:]), [knm])
        k2tm = em.op("V", lambda v: v.tensor_tensor(TMA.rearrange("p (h i) -> p h i", h=H), TMB.rearrange("p (h i) -> p h i", h=H), RSC[:, 16:32, None].broadcast_to([128, H, DH]), op=ALU.mult), [cft, knm, qnm])
        trk2 = transpose8(lambda j: TMA[:, j * 128:(j + 1) * 128], K2B, [k2tm])
        efr_d = em.dma("Q", EFRD[:], EFT[127:128, :], waits=[eft])
        ebc = em.dma("Q", EFEF[:], EFRD[0:1, :].broadcast_to([128, H]), waits=[efr_d])
        hb_toks = []
        for pr in range(8):
            hb_toks.append(em.op("S", lambda s, pr=pr: s.copy(HBD[0:64, pr, 0:64], ST1[0:64, pr, :]), [stb_tok]))
            hb_toks.append(em.op("S", lambda s, pr=pr: s.copy(HBD[64:128, pr, 64:128], ST1[64:128, pr, :]), [stb_tok]))
            hb_toks.append(em.op("S", lambda s, pr=pr: s.copy(HVD[0:64, pr, 0:64], ST2[0:64, pr, :]), [stb_tok]))
            hb_toks.append(em.op("S", lambda s, pr=pr: s.copy(HVD[64:128, pr, 64:128], ST2[64:128, pr, :]), [stb_tok]))
        hbt = hb_toks[-1]
        stm = []
        for h in range(16):
            rb = (h % 2) * 64
            cc = (h // 2) * 64
            w = [k3b, ktc, evv[-1]] + ([pg_g[1]] if h == 0 else [])
            stm.append(em.op("T", lambda t, h=h, rb=rb, cc=cc: t.matmul(
                PG[rb:rb + 64, 1, cc:cc + 64], K3TB[:, h * 64:h * 64 + 64], KTMB[:, h * 64:h * 64 + 64],
                start=True, stop=True), w))
            w2 = [k3b, evv[-1]] + ([pg_g[2]] if h == 0 else [])
            stm.append(em.op("T", lambda t, h=h, rb=rb, cc=cc: t.matmul(
                PG[rb:rb + 64, 2, cc:cc + 64], K3TB[:, h * 64:h * 64 + 64], VTB[:, h * 64:h * 64 + 64],
                start=True, stop=True), w2))
        sup = []
        for pr in range(8):
            for hf in range(2):
                rs_ = slice(64 * hf, 64 * hf + 64)
                h_ = 2 * pr + hf
                sup.append(em.op("V", lambda v, pr=pr, rs_=rs_, h_=h_: v.scalar_tensor_tensor(
                    ST1[rs_, pr, :], ST1[rs_, pr, :], EFEF[rs_, h_:h_ + 1], PG[rs_, 1, pr * 64:pr * 64 + 64],
                    op0=ALU.mult, op1=ALU.add), [stm[-1], ebc, hbt]))
                sup.append(em.op("V", lambda v, pr=pr, rs_=rs_, h_=h_: v.scalar_tensor_tensor(
                    ST2[rs_, pr, :], ST2[rs_, pr, :], EFEF[rs_, h_:h_ + 1], PG[rs_, 2, pr * 64:pr * 64 + 64],
                    op0=ALU.mult, op1=ALU.add), [stm[-1], ebc, hbt]))
        stb_tok = sup[-1]
        pg_g[1] = sup[-1]
        pg_g[2] = sup[-1]

        cgi1 = em.op("S", lambda s: s.copy(CGR[:], CGP[:]), [qnm])
        cgi2 = em.op("V", lambda v: v.memset(CGX[:], 0.0), [qnm])
        isq = em.op("S", lambda s: s.activation(SCR[:], CGP[:], AF.Square), [qnm, k3b])
        irs = em.op("V", lambda v: v.tensor_reduce(RSC[:, 0:16], SCR.rearrange("p (h i) -> p h i", h=H), axis=AX, op=ALU.add), [isq])

        ghl_tok = None
        lastT = None
        pprev = None
        for it in range(CG_ITERS + 1):
            final = (it == CG_ITERS)
            src = CGX if final else CGP
            dep = [eft, irs, pprev] if it == 0 else [eft, pprev]
            ph = em.op("V", lambda v, src=src: v.tensor_tensor(SCR.rearrange("p (h i) -> p h i", h=H), src.rearrange("p (h i) -> p h i", h=H), EFT[:, :, None].broadcast_to([128, H, DH]), op=ALU.mult), dep + ([cgi2] if final else []))
            pool = PT_DRAIN["PT"]
            trp = []
            gmms = []
            for j in range(8):
                tt = em.op("T", lambda t, j=j: t.transpose(PT[:, j % 4, :], SCR[:, j * 128:(j + 1) * 128], IDT[:]), [ph, pool[0]])
                ev = em.op("S", lambda s, j=j: s.copy(PHB[:, j, :], PT[:, j % 4, :]), [tt])
                pool[0] = ev
                trp.append(ev)
                for h in (2 * j, 2 * j + 1):
                    rb = (h % 2) * 64
                    w = [trk2[j], ev, gmms[-1] if gmms else None] + ([ghl_tok] if h % 4 == 0 else [])
                    gmms.append(em.op("T", lambda t, h=h, j=j, rb=rb: t.matmul(
                        PG[:, h // 4, (h % 4) * 128:(h % 4) * 128 + 128],
                        K2B[rb:rb + 64, j, :], PHB[rb:rb + 64, j, :], start=True, stop=True), w))
            mev = []
            for bb in range(2):
                mev.append(em.op("V", lambda v, bb=bb: v.tensor_tensor(
                    GHL[:, bb * 8:(bb + 1) * 8, :],
                    PG[:, bb * 2:(bb + 1) * 2, :].rearrange("p a b -> p (a b)").rearrange("p (a b) -> p a b", a=8),
                    TRI[:, None, :].broadcast_to([128, 8, 128]), op=ALU.mult), gmms + [wl1]))
            ghl_tok = mev[-1]
            amms = []
            for h in range(16):
                w = [mev[h // 8], ktc if not final else None] + ([pa_g[h // 8]] if h % 8 == 0 else [])
                amms.append(em.op("T", lambda t, h=h, final=final: t.matmul(
                    PA[:, h // 8, (h % 8) * 64:(h % 8) * 64 + 64], GHL[:, h, :],
                    (VTB if final else KTMB)[:, h * 64:h * 64 + 64], start=(h % 8 == 0), stop=False), w))
            for j in range(8):
                amms.append(em.op("T", lambda t, j=j, final=final: t.matmul(
                    PA[:, j // 4, (j % 4) * 128:(j % 4) * 128 + 128], PHB[:, j, :],
                    (HVD if final else HBD)[:, j, :], start=False, stop=(j % 4 == 3)), [trp[j], hbt]))
            lastT = amms[-1]
            if final:
                break
            lp = em.op("V", lambda v: v.tensor_tensor(SCR[:], CGP[:], LAMBB[:], op=ALU.mult), [trp[-1], lrt])
            ap1 = em.op("V", lambda v: v.tensor_add(TMA[:, 0:512], PA[:, 0, :], SCR[:, 0:512]), [lp, amms[-2], amms[-1]])
            ap2 = em.op("V", lambda v: v.tensor_add(TMA[:, 512:1024], PA[:, 1, :], SCR[:, 512:1024]), [lp, amms[-2], amms[-1]])
            pa_g[0] = ap1
            pa_g[1] = ap2
            pp1 = em.op("V", lambda v: v.tensor_mul(SCR[:], CGP[:], TMA[:]), [ap1, ap2])
            pp2 = em.op("V", lambda v: v.tensor_reduce(RSC[:, 16:32], SCR.rearrange("p (h i) -> p h i", h=H), axis=AX, op=ALU.add), [pp1])
            al1 = em.op("V", lambda v: v.tensor_scalar(RSC[:, 32:48], RSC[:, 16:32], 1e-12, None, op0=ALU.add), [pp2])
            al2 = em.op("V", lambda v: v.reciprocal(RSC[:, 32:48], RSC[:, 32:48]), [al1])
            al3 = em.op("V", lambda v: v.tensor_mul(RSC[:, 32:48], RSC[:, 32:48], RSC[:, 0:16]), [al2])
            x1 = em.op("V", lambda v: v.tensor_tensor(SCR.rearrange("p (h i) -> p h i", h=H), CGP.rearrange("p (h i) -> p h i", h=H), RSC[:, 32:48, None].broadcast_to([128, H, DH]), op=ALU.mult), [al3, pp1])
            x2 = em.op("V", lambda v: v.tensor_add(CGX[:], CGX[:], SCR[:]), [x1])
            r1 = em.op("V", lambda v: v.tensor_tensor(SCR.rearrange("p (h i) -> p h i", h=H), TMA.rearrange("p (h i) -> p h i", h=H), RSC[:, 32:48, None].broadcast_to([128, H, DH]), op=ALU.mult), [x2])
            r2 = em.op("V", lambda v: v.tensor_sub(CGR[:], CGR[:], SCR[:]), [r1, cgi1])
            rq1 = em.op("S", lambda s: s.activation(SCR[:], CGR[:], AF.Square), [r2])
            rq2 = em.op("V", lambda v: v.tensor_reduce(RSC[:, 48:64], SCR.rearrange("p (h i) -> p h i", h=H), axis=AX, op=ALU.add), [rq1])
            b1 = em.op("V", lambda v: v.tensor_scalar(RSC[:, 64:80], RSC[:, 0:16], 1e-12, None, op0=ALU.add), [rq2])
            b2 = em.op("V", lambda v: v.reciprocal(RSC[:, 64:80], RSC[:, 64:80]), [b1])
            b3 = em.op("V", lambda v: v.tensor_mul(RSC[:, 64:80], RSC[:, 64:80], RSC[:, 48:64]), [b2])
            rcp = em.op("V", lambda v: v.tensor_copy(RSC[:, 0:16], RSC[:, 48:64]), [b3])
            p1 = em.op("V", lambda v: v.tensor_tensor(SCR.rearrange("p (h i) -> p h i", h=H), CGP.rearrange("p (h i) -> p h i", h=H), RSC[:, 64:80, None].broadcast_to([128, H, DH]), op=ALU.mult), [b3, rq1])
            p2 = em.op("V", lambda v: v.tensor_add(CGP[:], CGR[:], SCR[:]), [p1])
            irs = rcp
            pprev = p2

        og1 = em.op("V", lambda v: v.tensor_mul(TMA[:, 0:512], PA[:, 0, :], GTF[:, 0:512]), [lastT] + gsg)
        og2 = em.op("V", lambda v: v.tensor_mul(TMA[:, 512:1024], PA[:, 1, :], GTF[:, 512:1024]), [lastT])
        pa_g[0] = og1
        pa_g[1] = og2
        osq = em.op("S", lambda s: s.activation(SCR[:], TMA[:], AF.Square), [og1, og2])
        oss = em.op("V", lambda v: v.tensor_reduce(RSC[:, 0:16], SCR.rearrange("p (h i) -> p h i", h=H), axis=AX, op=ALU.add), [osq])
        om1 = em.op("V", lambda v: v.tensor_scalar(RSC[:, 0:16], RSC[:, 0:16], 1.0 / DH, None, op0=ALU.mult), [oss])
        om2 = em.op("S", lambda s: s.activation(RSC[:, 16:32], RSC[:, 0:16], AF.Sqrt, bias=1e-5), [om1])
        om3 = em.op("V", lambda v: v.reciprocal(RSC[:, 16:32], RSC[:, 16:32]), [om2])
        zb = em.op("V", lambda v: v.tensor_tensor(TMB.rearrange("p (h i) -> p h i", h=H), TMA.rearrange("p (h i) -> p h i", h=H), RSC[:, 16:32, None].broadcast_to([128, H, DH]), op=ALU.mult), [om3])
        trz = transpose8(lambda j: TMB[:, j * 128:(j + 1) * 128], XNF, [zb])
        oml = []
        for j in range(8):
            for half in range(2):
                slot = (j * 2 + half) % 4
                w = [trz[j], WSG[slot]]
                ld = em.dma("Q", WS[:, slot, :], wb["ow"][j * 128:(j + 1) * 128, half * 512:(half + 1) * 512], waits=w)
                oml.append(em.op("T", lambda t, j=j, half=half, slot=slot: t.matmul(
                    PA[:, half, :], XNF[:, j, :], WS[:, slot, :], start=(j == 0), stop=(j == 7)),
                    [ld, pa_g[half] if j == 0 else None]))
                WSG[slot] = oml[-1]
        rt = []
        for half in range(2):
            rt.append(em.op("V", lambda v, half=half, i=i: v.tensor_add(
                X[:, i, half * 512:(half + 1) * 512], X[:, i, half * 512:(half + 1) * 512], PA[:, half, :]), [oml[-1]]))
        l1_res[i] = rt[-1]
        pa_g[0] = rt[0]
        pa_g[1] = rt[1]
        em.barrier()

    em.barrier()
    for cm in reversed(ctx_1):
        cm.__exit__(None, None, None)

    if phase_stop <= 4:
        return finalize_early([])

    ctx_2 = []
    XQ2 = enter(nc.sbuf_tensor("XQ2", [128, 8, 128], BF16), ctx_2)
    RSB = enter(nc.sbuf_tensor("RSB", [128, D], F32), ctx_2)
    rsl = em.dma("Q", RSB[:], RSRD[0:1, :].broadcast_to([128, D]))
    pa_g = [None, None]
    store_toks = []
    for i in range(NT):
        lt = ln_tile(i, [l1_res[i]])
        am = em.op("V", lambda v: v.reduce_max(STT[:, 8:9], XN[:], axis=AX, apply_absolute_value=True), [lt])
        am = em.op("V", lambda v: v.tensor_scalar(STT[:, 8:9], STT[:, 8:9], 1e-12, None, op0=ALU.max), [am])
        ls = em.op("V", lambda v: v.tensor_scalar(STT[:, 9:10], STT[:, 8:9], 1.0 / 127.0, None, op0=ALU.mult), [am])
        rc = em.op("V", lambda v: v.reciprocal(STT[:, 10:11], STT[:, 8:9]), [am])
        rc = em.op("V", lambda v: v.tensor_scalar(STT[:, 10:11], STT[:, 10:11], 127.0, None, op0=ALU.mult), [rc])
        q1 = em.op("V", lambda v: v.tensor_scalar(SCR[:], XN[:], STT[:, 10:11], MAGIC, op0=ALU.mult, op1=ALU.add), [rc])
        q2 = em.op("V", lambda v: v.tensor_scalar(SCR[:], SCR[:], MAGIC, 127.0, op0=ALU.subtract, op1=ALU.min), [q1])
        q3 = em.op("V", lambda v: v.tensor_scalar(TMB[:], SCR[:], -127.0, None, op0=ALU.max), [q2])
        trq = transpose8(lambda j: TMB[:, j * 128:(j + 1) * 128], XQ2, [q3])
        mms = []
        for j in range(8):
            for half in range(2):
                slot = (j * 2 + half) % 4
                w = [trq[j], WSG[slot]]
                ld = em.dma("Q", WS[:, slot, :], wb["rhs"][j * 128:(j + 1) * 128, half * 512:(half + 1) * 512], waits=w)
                mms.append(em.op("T", lambda t, j=j, half=half, slot=slot: t.matmul(
                    PA[:, half, :], XQ2[:, j, :], WS[:, slot, :], start=(j == 0), stop=(j == 7)),
                    [ld, pa_g[half] if j == 0 else None]))
                WSG[slot] = mms[-1]
        for half in range(2):
            t1 = em.op("V", lambda v, half=half: v.tensor_tensor(SCR[:, half * 512:(half + 1) * 512], PA[:, half, :], RSB[:, half * 512:(half + 1) * 512], op=ALU.mult), [mms[-1], rsl if i == 0 else None])
            t2 = em.op("V", lambda v, half=half, i=i: v.scalar_tensor_tensor(
                X[:, i, half * 512:(half + 1) * 512], SCR[:, half * 512:(half + 1) * 512], STT[:, 9:10],
                X[:, i, half * 512:(half + 1) * 512], op0=ALU.mult, op1=ALU.add), [t1, ls])
            pa_g[half] = t2
        st = em.dma("Q", OUT[:, i, :], X[:, i, :], waits=[pa_g[0], pa_g[1]])
        store_toks.append(st)
        em.barrier()

    em.op("V", lambda v: v.engine_nop() if False else v.memset(STT[:, 31:32], 0.0), [store_toks[-1]])
    for cm in reversed(ctx_2):
        cm.__exit__(None, None, None)

    with nc.Block() as block:
        em.install(block, sems)
    for cm in reversed(ctx_global):
        cm.__exit__(None, None, None)
    return nc


def host_layout(inputs):
    g = {}
    x = np.ascontiguousarray(inputs["L0_attn_query"], dtype=np.float32)
    B = x.shape[0]
    xs = [np.ascontiguousarray(x[b].reshape(NT, 128, D).transpose(1, 0, 2)) for b in range(B)]

    def fmcols(vec):
        return np.ascontiguousarray(np.asarray(vec, np.float32).reshape(8, 128).T)

    g["maax"] = fmcols(inputs["L0_attn_time_maa_x"])
    maacol = np.stack([fmcols(inputs["L0_attn_time_maa_w"]), fmcols(inputs["L0_attn_time_maa_k"]),
                       fmcols(inputs["L0_attn_time_maa_v"]), fmcols(inputs["L0_attn_time_maa_r"])], axis=1)
    g["maacol"] = np.ascontiguousarray(maacol)
    g["maaw1"] = np.ascontiguousarray(inputs["L0_attn_time_maa_w1"], np.float32)
    w2 = np.asarray(inputs["L0_attn_time_maa_w2"], np.float32)
    w2l = np.zeros((64, 16, 128), np.float32)
    for f in range(4):
        for j in range(8):
            w2l[32 * (f % 2):32 * (f % 2) + 32, (f // 2) * 8 + j, :] = w2[f, :, j * 128:(j + 1) * 128]
    g["w2l"] = np.ascontiguousarray(w2l)
    g["dw1"] = np.ascontiguousarray(inputs["L0_attn_time_decay_w1"], np.float32)
    dw2l = np.zeros((65, 8, 128), np.float32)
    td2 = np.asarray(inputs["L0_attn_time_decay_w2"], np.float32)
    for j in range(8):
        dw2l[0:64, j, :] = td2[:, j * 128:(j + 1) * 128]
        dw2l[64, j, :] = np.asarray(inputs["L0_attn_time_decay"], np.float32)[j * 128:(j + 1) * 128]
    g["dw2l"] = np.ascontiguousarray(dw2l)
    g["ufm"] = fmcols(np.asarray(inputs["L0_attn_time_faaaa"], np.float32).reshape(D))
    g["lnw"] = fmcols(inputs["L0_attn_ln_w"])
    g["lnb"] = fmcols(inputs["L0_attn_ln_b"])
    g["wr"] = np.ascontiguousarray(inputs["L0_attn_W_r"], np.float32)
    g["wk"] = np.ascontiguousarray(inputs["L0_attn_W_k"], np.float32)
    g["wv"] = np.ascontiguousarray(inputs["L0_attn_W_v"], np.float32)
    g["wo"] = np.ascontiguousarray(inputs["L0_attn_W_o"], np.float32)
    g["wgt"] = np.ascontiguousarray(np.asarray(inputs["L0_ffn_w_gate"], np.float32).T)
    g["wut"] = np.ascontiguousarray(np.asarray(inputs["L0_ffn_w_up"], np.float32).T)
    g["wdt"] = np.ascontiguousarray(np.asarray(inputs["L0_ffn_w_down"], np.float32).T)
    g["qw"] = np.ascontiguousarray(inputs["L1_attn_Wq"], np.float32)
    g["kw"] = np.ascontiguousarray(inputs["L1_attn_Wk"], np.float32)
    g["vw"] = np.ascontiguousarray(inputs["L1_attn_Wv"], np.float32)
    g["fw"] = np.ascontiguousarray(inputs["L1_attn_Wf"], np.float32)
    g["bw"] = np.ascontiguousarray(inputs["L1_attn_Wbet"], np.float32)
    g["ow"] = np.ascontiguousarray(inputs["L1_attn_Wo"], np.float32)
    g["delta"] = np.ascontiguousarray(np.asarray(inputs["L1_attn_delta"], np.float32).reshape(1, H))
    g["lambf"] = fmcols(inputs["L1_attn_lamb_params"])
    g["nrmw"] = fmcols(inputs["L1_attn_norm_weight"])
    g["gw1"] = np.ascontiguousarray(inputs["L1_attn_gate_w1"], np.float32)
    g["gw2"] = np.ascontiguousarray(inputs["L1_attn_gate_w2"], np.float32)
    g["rhs"] = np.ascontiguousarray(inputs["L1_ffn_rhs"], np.float32)
    g["rhst"] = np.ascontiguousarray(np.asarray(inputs["L1_ffn_rhs"], np.float32).T)
    g["idf"] = np.eye(128, dtype=np.float32)
    g["idu8"] = np.ascontiguousarray(np.eye(128, dtype=np.uint8))
    cv = np.zeros(8, np.float32)
    cv[:len(CONST_VALS)] = CONST_VALS
    g["consti"] = np.ascontiguousarray(np.tile(cv, (128, 1)))
    st = np.arange(128)
    g["trii"] = np.ascontiguousarray((st[:, None] <= st[None, :]).astype(np.float32))
    g["tris"] = np.ascontiguousarray((st[:, None] < st[None, :]).astype(np.float32))
    g["eyef"] = np.eye(128, dtype=np.float32)
    return g, xs


_NC_CACHE = {}


def kernel(**inputs):
    if "nc" not in _NC_CACHE:
        _NC_CACHE["nc"] = build_program()
    nc = _NC_CACHE["nc"]
    g, xs = host_layout(inputs)
    in_maps = []
    for b in range(8):
        m = dict(g)
        m["x0"] = xs[b]
        in_maps.append(m)
    res = run_bass_kernel_spmd(nc, in_maps, core_ids=list(range(8)))
    out = np.zeros((8, 4096, D), np.float32)
    for b in range(8):
        ob = res.results[b]["out"]
        out[b] = ob.transpose(1, 0, 2).reshape(4096, D)
    return out


# version 06 (final)
# speedup vs baseline: 1.1649x; 1.1649x over previous
import numpy as np
import ml_dtypes

import concourse.bass as bass
import concourse.mybir as mybir
from concourse.bass_utils import run_bass_kernel_spmd

F32 = mybir.dt.float32
BF16 = mybir.dt.bfloat16
AF = mybir.ActivationFunctionType
ALU = mybir.AluOpType
AX = mybir.AxisListType.X
BFD = ml_dtypes.bfloat16

D = 1024
NT = 32
H, DH = 16, 64
MAGIC = 12582912.0
CG_ITERS = 30
ENGINES = ("T", "V", "S", "G", "Q")
CONST_VALS = [1e-5, 0.00064, 1e-6, -1.0, 1.0, 1.0 / 4194304.0, 0.0, 2.0]


class Emit:
    def __init__(self):
        self.items = {e: [] for e in ENGINES}
        self.counts = {e: 0 for e in ENGINES}
        self.hwm = {e: {} for e in ENGINES}

    def op(self, eng, fn, waits=(), inc=1):
        w = []
        hw = self.hwm[eng]
        for t in waits:
            if t is None:
                continue
            se, sv = t
            if sv <= 0 or hw.get(se, 0) >= sv:
                continue
            if se == eng and sv > self.counts[eng]:
                continue
            hw[se] = sv
            w.append((se, sv))
        self.counts[eng] += inc
        self.items[eng].append((w, fn, inc))
        return (eng, self.counts[eng])

    def dma(self, q, out, in_, waits=()):
        w = list(waits) + [(q, self.counts[q])]
        return self.op(q, lambda e: e.dma_start(out=out, in_=in_), w, inc=16)

    def barrier(self):
        toks = [(e, self.counts[e]) for e in ENGINES]
        for e in ENGINES:
            self.op(e, lambda h: h.nop(), toks, inc=0 if e == "G" else 1)
        return [(e, self.counts[e]) for e in ENGINES]

    def install(self, block, sems):
        def mk(eng):
            items = self.items[eng]
            mysem = sems[eng]
            def runner(handle):
                for w, fn, inc in items:
                    for (se, sv) in w:
                        handle.wait_ge(sems[se], sv)
                    ins = fn(handle)
                    if ins is not None and inc > 0:
                        ins.then_inc(mysem, inc)
            return runner
        block.tensor(mk("T"))
        block.vector(mk("V"))
        block.scalar(mk("S"))
        block.gpsimd(mk("G"))
        block.sync(mk("Q"))


def build_program(debug=None, phase_stop=5, l0cut=9):
    nc = bass.Bass("TRN2", target_bir_lowering=False)
    em = Emit()

    def din(name, shape, dtype=F32):
        return nc.dram_tensor(name, list(shape), dtype, kind="ExternalInput").ap()

    X0 = din("x0", [128, NT, D])
    MAAX = din("maax", [128, 8])
    MAACOL = din("maacol", [128, 4, 8])
    MAAW1 = din("maaw1", [D, 128])
    W2L = din("w2l", [64, 16, 128])
    DW1 = din("dw1", [D, DH])
    DW2L = din("dw2l", [65, 8, 128])
    UFM = din("ufm", [128, 8])
    LNW = din("lnw", [128, 8])
    LNB = din("lnb", [128, 8])
    WR = din("wr", [D, D])
    WK = din("wk", [D, D])
    WV = din("wv", [D, D])
    WO = din("wo", [D, D])
    WGT = din("wgt", [D, 4096])
    WUT = din("wut", [D, 4096])
    WDT = din("wdt", [4096, D])
    QW = din("qw", [D, D])
    KW = din("kw", [D, D])
    VW = din("vw", [D, D])
    FW = din("fw", [D, H])
    BW = din("bw", [D, H])
    OW = din("ow", [D, D])
    DELTA = din("delta", [1, H])
    LAMBF = din("lambf", [128, 8])
    NRMW = din("nrmw", [128, 8])
    GW1 = din("gw1", [D, DH])
    GW2 = din("gw2", [DH, D])
    RHS = din("rhs", [D, D])
    RHST = din("rhst", [D, D])
    IDF = din("idf", [128, 128])
    IDU8 = din("idu8", [128, 128], mybir.dt.uint8)
    CONSTI = din("consti", [128, 8])
    TRII = din("trii", [128, 128])
    TRIS = din("tris", [128, 128])
    EYEF = din("eyef", [128, 128])

    OUT = nc.dram_tensor("out", [128, NT, D], F32, kind="ExternalOutput").ap()
    DBG = None
    if debug is not None:
        DBG = nc.dram_tensor("dbg", list(debug), F32, kind="ExternalOutput").ap()

    wb = {}
    for nm, shape in [("wr", [D, D]), ("wk", [D, D]), ("wv", [D, D]), ("wo", [D, D]),
                      ("g", [D, 4096]), ("u", [D, 4096]), ("d", [4096, D]),
                      ("qw", [D, D]), ("kw", [D, D]), ("vw", [D, D]), ("ow", [D, D]),
                      ("rhs", [D, D])]:
        wb[nm] = nc.dram_tensor("wb_" + nm, shape, BF16, kind="Internal").ap()
    LAMBD = nc.dram_tensor("lamb_dram", [1, D], F32, kind="Internal").ap()
    RSRD = nc.dram_tensor("rsrow_dram", [1, D], F32, kind="Internal").ap()
    SCLD = nc.dram_tensor("scl_dram", [1, 4], F32, kind="Internal").ap()
    ROWD = nc.dram_tensor("row_dram", [1, D], F32, kind="Internal").ap()
    EFRD = nc.dram_tensor("efr_dram", [1, H], F32, kind="Internal").ap()

    ctx_global = []
    def enter(cm, store):
        t = cm.__enter__()
        store.append(cm)
        return t

    sems = {}
    for e in ENGINES:
        sems[e] = enter(nc.semaphore("sem_" + e), ctx_global)

    def sbg(name, shape, dt=F32):
        return enter(nc.sbuf_tensor(name, list(shape), dt), ctx_global)
    def psg(name, shape, dt=F32):
        return enter(nc.psum_tensor(name, list(shape), dt), ctx_global)

    X = sbg("X", [128, NT, D])
    IDT = sbg("IDT", [128, 128])
    IDB = sbg("IDB", [128, 128], BF16)
    IDU = sbg("IDU", [128, 128], mybir.dt.uint8)
    CST = sbg("CST", [128, 80])
    LAMB = sbg("LAMB", [128, 8])
    CPA = sbg("CPA", [128, 8])
    for _ci, _cv in enumerate(CONST_VALS):
        nc.const_aps.register(F32, _cv, CPA[:, _ci:_ci + 1])
    MGB = sbg("MGB", [128, 4])
    STT = sbg("STT", [128, 32])
    XN = sbg("XN", [128, D])
    SCR = sbg("SCR", [128, D])
    TMA = sbg("TMA", [128, D])
    TMB = sbg("TMB", [128, D])
    TB1 = sbg("TB1", [128, D], BF16)
    TB2 = sbg("TB2", [128, D], BF16)
    WS = sbg("WS", [128, 4, 512], BF16)
    WSG = [None, None, None, None]
    ST1 = sbg("ST1", [128, 8, DH])
    ST2 = sbg("ST2", [128, 8, DH])
    ctx_mid = []
    def sbm(name, shape, dt=F32):
        return enter(nc.sbuf_tensor(name, list(shape), dt), ctx_mid)
    TRS = sbm("TRS", [128, 128])
    BIASR = sbm("BIASR", [1, D])
    ONEC = sbm("ONEC", [1, 128])
    ST1B = sbm("ST1B", [128, 8, DH], BF16)

    PT = psg("PT", [128, 4, 128])
    PT_DRAIN = {"PT": [None]}
    PG = psg("PG", [128, 4, 512])
    PA = psg("PA", [128, 2, 512])

    qz = em.dma("Q", X[:], X0[:])
    em.dma("Q", CPA[:], CONSTI[:])
    em.dma("Q", IDU[:], IDU8[:])
    em.dma("Q", IDT[:], IDF[:])
    em.dma("Q", TRS[:], TRIS[:])
    em.dma("Q", CST[:, 0:8], MAAX[:])
    em.dma("Q", CST[:, 8:40], MAACOL.rearrange("p f j -> p (f j)"))
    em.dma("Q", CST[:, 40:48], UFM[:])
    em.dma("Q", CST[:, 48:56], LNW[:])
    em.dma("Q", CST[:, 56:64], LNB[:])
    em.dma("Q", CST[:, 64:72], LAMBF[:])
    em.dma("Q", CST[:, 72:80], NRMW[:])
    q0 = ("Q", em.counts["Q"])
    em.op("V", lambda v: v.memset(ONEC[:], 1.0), [])
    em.op("S", lambda s: s.copy(IDB[:], IDT[:]), [q0])
    tl0 = em.op("S", lambda s: s.activation(STT[:, 0:8], CST[:, 64:72], AF.Exp), [q0])
    tl = em.op("S", lambda s: s.activation(STT[:, 0:8], STT[:, 0:8], AF.Ln, bias=1.0), [tl0])
    em.op("V", lambda v: v.tensor_scalar(LAMB[:], STT[:, 0:8], 0.25, None, op0=ALU.add), [tl])

    ctx_pre = []
    def sbp(name, shape, dt=F32):
        return enter(nc.sbuf_tensor(name, list(shape), dt), ctx_pre)
    PRE = sbp("PRE", [128, 2, 1024])
    PREB = sbp("PREB", [128, 2, 1024], BF16)
    PST = sbp("PST", [128, 36])
    ROWB = sbp("ROWB", [128, 1024])
    ROW8 = sbp("ROW8", [8, 128])
    PADP = sbp("PADP", [128, 128])
    padz = em.op("V", lambda v: v.memset(PADP[:], 0.0), [])

    pre_rd = {0: None, 1: None}

    def cast_w(src, dst, row_scale=None, waits=()):
        last = None
        for j in range(8):
            b = j % 2
            w = list(waits) + [pre_rd[b]]
            ld = em.dma("Q", PRE[:, b, :], src[j * 128:(j + 1) * 128, :], waits=w)
            if row_scale is None:
                ct = em.op("S", lambda s, b=b: s.copy(PREB[:, b, :], PRE[:, b, :]), [ld])
            else:
                ct = em.op("S", lambda s, b=b, j=j: s.activation(PREB[:, b, :], PRE[:, b, :], AF.Copy, scale=row_scale(j)), [ld])
            pre_rd[b] = ct
            st = em.dma("Q", dst[j * 128:(j + 1) * 128, :], PREB[:, b, :], waits=[ct])
            last = st
        return last

    cast_w(WR, wb["wr"], waits=[q0])
    cast_w(WK, wb["wk"])
    cast_w(WV, wb["wv"])
    cast_w(WO, wb["wo"], row_scale=lambda j: CST[:, 48 + j:49 + j])
    cast_w(QW, wb["qw"])
    cast_w(KW, wb["kw"])
    cast_w(VW, wb["vw"])
    cast_w(OW, wb["ow"], row_scale=lambda j: CST[:, 72 + j:73 + j])

    for j in range(8):
        b = j % 2
        ld = em.dma("Q", PRE[:, b, :], WO[j * 128:(j + 1) * 128, :], waits=[pre_rd[b]])
        for n in range(2):
            em.op("T", lambda t, j=j, n=n, b=b: t.matmul(
                PA[0:1, 0, n * 512:(n + 1) * 512] if n == 0 else PA[0:1, 1, 0:512],
                CST[:, 56 + j:57 + j], PRE[:, b, n * 512:(n + 1) * 512],
                start=(j == 0), stop=(j == 7)), [ld, q0])
        pre_rd[b] = ("T", em.counts["T"])
    tb = ("T", em.counts["T"])
    em.op("S", lambda s: s.copy(BIASR[0:1, 0:512], PA[0:1, 0, 0:512]), [tb])
    em.op("S", lambda s: s.copy(BIASR[0:1, 512:1024], PA[0:1, 1, 0:512]), [tb])

    def quant_bl(srcT, dst, nrow, ncolslices, idx):
        sums = []
        for j in range(nrow):
            for cs in range(ncolslices):
                b = (j * ncolslices + cs) % 2
                ld = em.dma("Q", PRE[:, b, :], srcT[j * 128:(j + 1) * 128, cs * 1024:(cs + 1) * 1024], waits=[pre_rd[b]])
                rd = em.op("V", lambda v, b=b, j=j, cs=cs: v.reduce_sum(
                    PST[:, (j * ncolslices + cs):(j * ncolslices + cs) + 1],
                    PRE[:, b, :], axis=AX, apply_absolute_value=True), [ld])
                pre_rd[b] = rd
                sums.append((rd, ld))
        nsl = nrow * ncolslices
        rall = em.op("V", lambda v: v.reduce_sum(PST[:, 32:33], PST[:, 0:nsl], axis=AX), [s[0] for s in sums])
        on = em.op("V", lambda v: v.memset(PST[:, 33:34], 1.0), [])
        mm = em.op("T", lambda t: t.matmul(PG[0:1, 0, 0:1], PST[:, 32:33], PST[:, 33:34], start=True, stop=True), [rall, on, tb])
        sc = em.op("S", lambda s: s.activation(PST[0:1, 34:35], PG[0:1, 0, 0:1], AF.Copy, scale=1.0 / (nsl * 128.0 * 1024.0)), [mm])
        sc = em.op("V", lambda v: v.tensor_scalar(PST[0:1, 34:35], PST[0:1, 34:35], 1e-5, None, op0=ALU.max), [sc])
        mg1 = em.dma("Q", SCLD[0:1, idx:idx + 1], PST[0:1, 34:35], waits=[sc])
        mg = em.dma("Q", MGB[:, idx:idx + 1], SCLD[0:1, idx:idx + 1].broadcast_to([128, 1]), waits=[mg1])
        rcs = em.op("V", lambda v: v.reciprocal(PST[:, 35:36], MGB[:, idx:idx + 1]), [mg])
        lastq = None
        for j in range(nrow):
            for cs in range(ncolslices):
                b = (j * ncolslices + cs) % 2
                ld = em.dma("Q", PRE[:, b, :], srcT[j * 128:(j + 1) * 128, cs * 1024:(cs + 1) * 1024], waits=[rcs, pre_rd[b]])
                t1 = em.op("V", lambda v, b=b: v.tensor_scalar(PRE[:, b, :], PRE[:, b, :], PST[:, 35:36], MAGIC, op0=ALU.mult, op1=ALU.add), [ld])
                t2 = em.op("V", lambda v, b=b: v.tensor_scalar(PRE[:, b, :], PRE[:, b, :], MAGIC, 1.0, op0=ALU.subtract, op1=ALU.min), [t1])
                t3 = em.op("V", lambda v, b=b: v.tensor_scalar(PREB[:, b, :], PRE[:, b, :], -1.0, None, op0=ALU.max), [t2])
                pre_rd[b] = t3
                st = em.dma("Q", dst[j * 128:(j + 1) * 128, cs * 1024:(cs + 1) * 1024], PREB[:, b, :], waits=[t3])
                lastq = st
        return lastq

    quant_bl(WGT, wb["g"], 8, 4, 0)
    quant_bl(WUT, wb["u"], 8, 4, 1)
    quant_bl(WDT, wb["d"], 32, 1, 2)

    rp_state = [padz]
    def rowpack(src_ap, dst_row, deps):
        pool = PT_DRAIN["PT"]
        cp = em.op("V", lambda v: v.tensor_copy(PADP[:, 0:8], src_ap), deps + [rp_state[0]])
        tt = em.op("T", lambda t: t.transpose(PT[:, 0, :], PADP[:], IDT[:]), [cp, tb, pool[0]])
        ev = em.op("S", lambda s: s.copy(ROW8[:], PT[0:8, 0, :]), [tt])
        pool[0] = ev
        rp_state[0] = ev
        return em.dma("Q", dst_row, ROW8[:], waits=[ev])

    ams = []
    for j in range(8):
        b = j % 2
        ld = em.dma("Q", PRE[:, b, :], RHST[j * 128:(j + 1) * 128, :], waits=[pre_rd[b]])
        am = em.op("V", lambda v, b=b, j=j: v.reduce_max(PST[:, j:j + 1], PRE[:, b, :], axis=AX, apply_absolute_value=True), [ld])
        pre_rd[b] = am
        ams.append((am, ld))
    amt = em.op("V", lambda v: v.tensor_scalar(PST[:, 8:16], PST[:, 0:8], 1e-12, None, op0=ALU.max), [a[0] for a in ams])
    s1 = em.op("V", lambda v: v.tensor_scalar(PST[:, 16:24], PST[:, 8:16], 1.0 / 127.0, None, op0=ALU.mult), [amt])
    cpr = rowpack(PST[:, 16:24], RSRD[:], [s1])
    rc = em.op("V", lambda v: v.reciprocal(PST[:, 24:32], PST[:, 8:16]), [amt])
    rc = em.op("V", lambda v: v.tensor_scalar(PST[:, 24:32], PST[:, 24:32], 127.0, None, op0=ALU.mult), [rc])
    cp2 = rowpack(PST[:, 24:32], ROWD[:], [rc, cpr])
    rwb = em.dma("Q", ROWB[:], ROWD[0:1, :].broadcast_to([128, D]), waits=[cp2])
    for j in range(8):
        b = j % 2
        ld = em.dma("Q", PRE[:, b, :], RHS[j * 128:(j + 1) * 128, :], waits=[rwb, pre_rd[b]])
        t1 = em.op("V", lambda v, b=b: v.tensor_tensor(PRE[:, b, :], PRE[:, b, :], ROWB[:], op=ALU.mult), [ld])
        t2 = em.op("V", lambda v, b=b: v.tensor_scalar(PRE[:, b, :], PRE[:, b, :], MAGIC, MAGIC, op0=ALU.add, op1=ALU.subtract), [t1])
        t3 = em.op("V", lambda v, b=b: v.tensor_scalar(PRE[:, b, :], PRE[:, b, :], 127.0, -127.0, op0=ALU.min, op1=ALU.max), [t2])
        t4 = em.op("S", lambda s, b=b: s.copy(PREB[:, b, :], PRE[:, b, :]), [t3])
        pre_rd[b] = t4
        st = em.dma("Q", wb["rhs"][j * 128:(j + 1) * 128, :], PREB[:, b, :], waits=[t4])

    em.barrier()
    for cm in reversed(ctx_pre):
        cm.__exit__(None, None, None)

    def finalize_early(open_ctxs):
        em.barrier()
        sts = []
        for i in range(NT):
            sts.append(em.dma("Q", OUT[:, i, :], X[:, i, :]))
        em.op("V", lambda v: v.memset(STT[:, 31:32], 0.0), [sts[-1]])
        for ctx in open_ctxs:
            for cm in reversed(ctx):
                cm.__exit__(None, None, None)
        with nc.Block() as block:
            em.install(block, sems)
        for cm in reversed(ctx_global):
            cm.__exit__(None, None, None)
        return nc

    if phase_stop <= 1:
        return finalize_early([ctx_mid])

    def ln_tile(i, deps):
        x_ap = X[:, i, :]
        t1 = em.op("V", lambda v: v.reduce_sum(STT[:, 0:1], x_ap, axis=AX), deps)
        t2 = em.op("S", lambda s: s.activation(SCR[:], x_ap, AF.Square), deps)
        t3 = em.op("V", lambda v: v.reduce_sum(STT[:, 1:2], SCR[:], axis=AX), [t2])
        t4 = em.op("V", lambda v: v.tensor_scalar(STT[:, 2:3], STT[:, 0:1], -1.0 / D, None, op0=ALU.mult), [t1])
        t5 = em.op("V", lambda v: v.tensor_tensor(STT[:, 4:5], STT[:, 2:3], STT[:, 2:3], op=ALU.mult), [t4])
        t6 = em.op("V", lambda v: v.tensor_scalar(STT[:, 5:6], STT[:, 1:2], 1.0 / D, STT[:, 4:5], op0=ALU.mult, op1=ALU.subtract), [t3, t5])
        t7 = em.op("S", lambda s: s.activation(STT[:, 6:7], STT[:, 5:6], AF.Sqrt, bias=1e-5), [t6])
        t8 = em.op("V", lambda v: v.reciprocal(STT[:, 3:4], STT[:, 6:7]), [t7])
        t9 = em.op("V", lambda v: v.tensor_scalar(XN[:], x_ap, STT[:, 2:3], STT[:, 3:4], op0=ALU.add, op1=ALU.mult), [t8])
        return t9

    def transpose8(in_fn, out3, deps, n=8, ident=None):
        pool = PT_DRAIN["PT"]
        evs = []
        for j in range(n):
            pj = PT[:, j % 4, :]
            w = list(deps) + [pool[0]]
            tt = em.op("T", lambda t, j=j, pj=pj: t.transpose(pj, in_fn(j), IDT[:]), w)
            ev = em.op("S", lambda s, pj=pj, j=j: s.copy(out3[:, j, :], pj), [tt])
            pool[0] = ev
            evs.append(ev)
        return evs

    def transpose1(pool_name, pj, out_ap, in_ap, idn_ap, deps, ev_src=None):
        pool = PT_DRAIN["PT"]
        tt = em.op("T", lambda t: t.transpose(pj, in_ap, idn_ap), list(deps) + [pool[0]])
        src = ev_src if ev_src is not None else pj
        ev = em.op("S", lambda s: s.copy(out_ap, src), [tt])
        pool[0] = ev
        return ev


    ctx_l0 = []
    def sb0(name, shape, dt=F32):
        return enter(nc.sbuf_tensor(name, list(shape), dt), ctx_l0)
    SMW = sb0("SMW", [128, 8, 128], BF16)
    W2B = sb0("W2B", [64, 16, 128], BF16)
    DW1B = sb0("DW1B", [128, 8, DH], BF16)
    DW2B = sb0("DW2B", [65, 8, 128], BF16)
    em.dma("G", SMW[:], MAAW1.rearrange("(j p) l -> p j l", p=128))
    em.dma("G", W2B[:], W2L[:])
    em.dma("G", DW1B[:], DW1.rearrange("(j p) l -> p j l", p=128))
    em.dma("G", DW2B[:], DW2L[:])
    wl0 = ("G", em.counts["G"])
    FMA = sb0("FMA", [128, 8, 128])
    FMB = sb0("FMB", [128, 8, 128])
    FMC = sb0("FMC", [128, 8, 128])
    FME = sb0("FME", [128, 8, 128])
    MFM = sb0("MFM", [64, 2, 128], BF16)
    DL1B = sb0("DL1B", [65, 128], BF16)
    HALO = sb0("HALO", [128, 8])
    XWB = sb0("XWB", [128, 8, 128], BF16)
    XKB = sb0("XKB", [128, 8, 128], BF16)
    XVB = sb0("XVB", [128, 8, 128], BF16)
    XRB = sb0("XRB", [128, 8, 128], BF16)
    KTB = sb0("KTB", [128, 8, 128], BF16)
    RBB = sb0("RBB", [128, 8, 128], BF16)
    GHB = sb0("GHB", [128, 8, 128], BF16)
    KHT = TB2.rearrange("p (a b) -> p a b", a=8)
    RTB = XRB
    UKB = XKB
    KHB = XVB
    ZFB = XWB

    em.op("V", lambda v: v.memset(ST1.rearrange("p a b -> p (a b)"), 0.0), [])
    em.op("S", lambda s: s.copy(ST1B.rearrange("p a b -> p (a b)"), ST1.rearrange("p a b -> p (a b)")), [("V", em.counts["V"])])
    em.op("V", lambda v: v.memset(HALO[:], 0.0), [])
    em.op("V", lambda v: v.memset(DL1B.rearrange("p c -> p c"), 0.0), [])
    em.op("V", lambda v: v.memset(DL1B[64:65, :], 1.0), [("V", em.counts["V"])])
    z0 = ("V", em.counts["V"])
    s0tok = ("S", em.counts["S"])

    pg_last_read = [None, None, None, None]
    pa_last_read = [None, None]
    pt_guard = None

    resid_tok = {}

    for i in range(NT):
        lt = ln_tile(i, [qz])
        if l0cut <= 0.05:
            em.barrier()
            continue
        tev = transpose8(lambda j: XN[:, j * 128:(j + 1) * 128], FMA[:, :, 0:128], [lt])
        if l0cut <= 0.1:
            em.barrier()
            continue
        xx1 = em.op("V", lambda v: v.tensor_sub(FMB[:, :, 1:128], FMA[:, :, 0:127], FMA[:, :, 1:128]), tev)
        xx0 = em.op("V", lambda v: v.tensor_sub(FMB[:, :, 0:1], HALO[:, :, None], FMA[:, :, 0:1]), tev + [z0])
        hup = em.op("V", lambda v: v.tensor_copy(HALO[:, :, None], FMA[:, :, 127:128]), [xx1, xx0])
        z0 = hup
        if l0cut <= 0.15:
            em.barrier()
            continue
        xxt = [xx1, xx0]
        if l0cut <= 0.2:
            em.barrier()
            continue
        x3 = []
        for j in range(8):
            x3.append(em.op("V", lambda v, j=j: v.scalar_tensor_tensor(
                KTB[:, j, :], FMB[:, j, :], CST[:, j:j + 1], FMA[:, j, 0:128],
                op0=ALU.mult, op1=ALU.add), xxt + tev))
        mw = [pg_last_read[0], wl0]
        mmm = []
        for f in range(4):
            pb = 32 * (f % 2)
            for j in range(8):
                mmm.append(em.op("T", lambda t, f=f, j=j, pb=pb: t.matmul(
                    PG[pb:pb + 32, 0, (f // 2) * 128:(f // 2) * 128 + 128], SMW[:, j, f * 32:(f + 1) * 32], KTB[:, j, :],
                    start=(j == 0), stop=(j == 7)), [x3[j]] + mw))
        mtanh = em.op("S", lambda s: s.activation(MFM.rearrange("p a b -> p (a b)"), PG[0:64, 0, 0:256], AF.Tanh), [mmm[-1]])
        pg_last_read[0] = mtanh
        dsts = [XWB, XKB, XVB, XRB]
        mixtok = {0: [], 1: [], 2: [], 3: []}
        stt_prev = []
        for f in range(4):
            for j in range(8):
                bank = 1 + ((f * 8 + j) % 2)
                pj = PG[:, bank, 0:128]
                w = [mtanh, wl0] + ([stt_prev[-2]] if len(stt_prev) >= 2 else [])
                if pg_last_read[bank] is not None:
                    w.append(pg_last_read[bank])
                t1 = em.op("T", lambda t, f=f, j=j, pj=pj: t.matmul(pj, W2B[32 * (f % 2):32 * (f % 2) + 32, (f // 2) * 8 + j, :], MFM[32 * (f % 2):32 * (f % 2) + 32, f // 2, :], start=True, stop=True), w)
                t2 = em.op("V", lambda v, f=f, j=j, pj=pj: v.scalar_tensor_tensor(
                    FMC[:, j, :], pj, CST[:, 8 + f * 8 + j:9 + f * 8 + j], FMB[:, j, :], op0=ALU.add, op1=ALU.mult), [t1])
                pg_last_read[bank] = t2
                stt_prev.append(t2)
                t3 = em.op("V", lambda v, f=f, j=j, dst=dsts[f]: v.tensor_add(dst[:, j, :], FMA[:, j, 0:128], FMC[:, j, :]), [t2])
                mixtok[f].append(t3)
        if l0cut <= 0.4:
            em.barrier()
            continue
        w = [pg_last_read[3], wl0]
        for j in range(8):
            em.op("T", lambda t, j=j: t.matmul(PG[0:DH, 3, 0:128], DW1B[:, j, :], XWB[:, j, :], start=(j == 0), stop=(j == 7)), [mixtok[0][j]] + w)
        tmm = ("T", em.counts["T"])
        dtanh = em.op("S", lambda s: s.activation(DL1B[0:64, :], PG[0:DH, 3, 0:128], AF.Tanh), [tmm])
        pg_last_read[3] = dtanh
        wwm = []
        for j in range(8):
            pj = PA[:, j // 4, (j % 4) * 128:(j % 4) * 128 + 128]
            w = [dtanh, wl0] + [pa_last_read[j // 4]]
            wwm.append(em.op("T", lambda t, j=j, pj=pj: t.matmul(pj, DW2B[:, j, :], DL1B[:], start=True, stop=True), w))
        if l0cut <= 0.7:
            em.barrier()
            continue
        ew = []
        for half in range(2):
            ew.append(em.op("S", lambda s, half=half: s.activation(
                FMC[:, 4 * half:4 * half + 4, :].rearrange("p a b -> p (a b)"),
                PA[:, half, :], AF.Exp), [wwm[4 * half + 3], wwm[-1], mixtok[3][7]]))
        pa_last_read[0] = ew[0]
        pa_last_read[1] = ew[1]
        sc = []
        for j in range(8):
            sc.append(em.op("V", lambda v, j=j: v.tensor_tensor_scan(
                FME[:, j, :], FMC[:, j, :], FMC[:, j, :], 0.0, op0=ALU.add, op1=ALU.bypass), ew))
        pex = em.op("S", lambda s: s.activation(FMC.rearrange("p a b -> p (a b)"), FME.rearrange("p a b -> p (a b)"), AF.Exp), sc)
        nex = em.op("S", lambda s: s.activation(FMA[:, :, 0:128].rearrange("p a b -> p (a b)"), FME.rearrange("p a b -> p (a b)"), AF.Exp, scale=-1.0), sc)

        if l0cut <= 1:
            em.barrier()
            continue
        def tmproj(srcb, wdr, mix_t, out_tm_fp32, out_bf=None):
            mms = []
            for j in range(8):
                for half in range(2):
                    slot = (j * 2 + half) % 4
                    w = list(mix_t) + [WSG[slot]]
                    ld = em.dma("Q" if half == 0 else "G", WS[:, slot, :], wdr[j * 128:(j + 1) * 128, half * 512:(half + 1) * 512], waits=w)
                    wm = [ld] + [pa_last_read[half]]
                    mms.append(em.op("T", lambda t, j=j, half=half, slot=slot: t.matmul(
                        PA[:, half, :], srcb[:, j, :], WS[:, slot, :],
                        start=(j == 0), stop=(j == 7)), wm))
                    WSG[slot] = mms[-1]
            evs = []
            for half in range(2):
                if out_tm_fp32 is not None:
                    evs.append(em.op("S", lambda s, half=half: s.copy(out_tm_fp32[:, half * 512:(half + 1) * 512], PA[:, half, :]), [mms[-1]]))
                else:
                    evs.append(em.op("S", lambda s, half=half: s.copy(out_bf[:, half * 512:(half + 1) * 512], PA[:, half, :]), [mms[-1]]))
            pa_last_read[0] = evs[0]
            pa_last_read[1] = evs[1]
            return evs

        evr = tmproj(XRB, wb["wr"], mixtok[3], TMA)
        trr = transpose8(lambda j: TMA[:, j * 128:(j + 1) * 128], FMB, evr)
        evk = tmproj(XKB, wb["wk"], mixtok[1], TMB)
        trk = transpose8(lambda j: TMB[:, j * 128:(j + 1) * 128], FME, evk)
        evv = tmproj(XVB, wb["wv"], mixtok[2], None, out_bf=TB1)

        if l0cut <= 2:
            em.barrier()
            continue
        ktb = em.op("V", lambda v: v.tensor_mul(KTB.rearrange("p a b -> p (a b)"), FME.rearrange("p a b -> p (a b)"), FMC.rearrange("p a b -> p (a b)")), trk + [pex, mmm[-1]])
        rtb1 = em.op("V", lambda v: v.tensor_mul(RTB[:, :, 1:128], FMB[:, :, 1:128], FMA[:, :, 0:127]), trr + [nex])
        rtb0 = em.op("V", lambda v: v.tensor_copy(RTB[:, :, 0:1], FMB[:, :, 0:1]), trr)
        ukb = em.op("V", lambda v: v.tensor_tensor(UKB[:], FME[:], CST[:, 40:48, None].broadcast_to([128, 8, 128]), op=ALU.mult), trk)
        rbb = em.op("S", lambda s: s.copy(RBB.rearrange("p a b -> p (a b)"), FMB.rearrange("p a b -> p (a b)")), trr + [rtb1, rtb0])
        kh1 = em.op("V", lambda v: v.tensor_tensor(FMC[:], FMC[:], FMA[:, :, 127:128].broadcast_to([128, 8, 128]), op=ALU.mult), [pex, nex, ktb])
        kh2 = em.op("V", lambda v: v.tensor_mul(FMB.rearrange("p a b -> p (a b)"), FME.rearrange("p a b -> p (a b)"), FMC.rearrange("p a b -> p (a b)")), [kh1, evv[-1], rbb, rtb1, rtb0])
        trkh = transpose8(lambda j: FMB[:, j, :], KHT, [kh2])

        if l0cut <= 3:
            em.barrier()
            continue
        ytoks = []
        ymm_all = []
        prev_half_tok = [None]
        for half in range(2):
            g1m = []
            g2m = []
            gprev = [None]
            for hh in range(8):
                h = half * 8 + hh
                j = h // 2
                rb = (h % 2) * 64
                p1 = PG[:, hh // 4, (hh % 4) * 128:(hh % 4) * 128 + 128]
                p2 = PG[:, 2 + hh // 4, (hh % 4) * 128:(hh % 4) * 128 + 128]
                w1 = [ktb, rtb1, rtb0, pg_last_read[hh // 4], trkh[-1], gprev[0]]
                if l0cut == 3.03:
                    g1m.append(em.op("T", lambda t, p1=p1: t.matmul(
                        p1, KTB[0:64, 0, :], KTB[0:64, 0, :], start=True, stop=True), w1))
                elif l0cut == 3.01:
                    g1m.append(em.op("T", lambda t: t.matmul(
                        PG[:, 0, 0:128], KTB[0:64, 0, :], KTB[0:64, 0, :], start=True, stop=True), w1))
                elif l0cut == 3.04:
                    g1m.append(em.op("T", lambda t, hh=hh: t.matmul(
                        PG[:, 0, 0:128], KTB[0:64, 0, :], KTB[0:64, 0, :], start=(hh % 8 == 0), stop=(hh % 8 == 7)), w1))
                elif l0cut == 3.06:
                    g1m.append(em.op("T", lambda t, p1=p1: t.matmul(
                        p1, KTB[0:64, 0, :], KTB[0:64, 0, :], start=True, stop=True), w1 + [g1m[-1] if g1m else None]))
                elif l0cut == 3.02:
                    pa_t = PA[:, hh // 4, (hh % 4) * 128:(hh % 4) * 128 + 128]
                    g1m.append(em.op("T", lambda t, pa_t=pa_t: t.matmul(
                        pa_t, KTB[0:64, 0, :], KTB[0:64, 0, :], start=True, stop=True), w1))
                elif l0cut != 3.08:
                    g1m.append(em.op("T", lambda t, j=j, rb=rb, p1=p1: t.matmul(
                        p1, KTB[rb:rb + 64, j, :], RTB[rb:rb + 64, j, :], start=True, stop=True), w1))
                    gprev[0] = g1m[-1]
                w2 = [ukb, rbb, pg_last_read[2 + hh // 4], gprev[0]]
                if l0cut != 3.05:
                    g2m.append(em.op("T", lambda t, j=j, rb=rb, p2=p2: t.matmul(
                        p2, UKB[rb:rb + 64, j, :], RBB[rb:rb + 64, j, :], start=True, stop=True), w2))
                    gprev[0] = g2m[-1]
            if l0cut <= 3.2:
                continue
            ma = em.op("V", lambda v: v.tensor_tensor(
                SCR.rearrange("p (a b) -> p a b", a=8),
                PG[:, 0:2, :].rearrange("p a b -> p (a b)").rearrange("p (a b) -> p a b", a=8),
                TRS[:, None, :].broadcast_to([128, 8, 128]), op=ALU.mult),
                g1m + [lt, prev_half_tok[0]])
            mb = em.op("V", lambda v: v.tensor_tensor(
                TMA.rearrange("p (a b) -> p a b", a=8),
                PG[:, 2:4, :].rearrange("p a b -> p (a b)").rearrange("p (a b) -> p a b", a=8),
                IDT[:, None, :].broadcast_to([128, 8, 128]), op=ALU.mult),
                g2m + [trr[-1], prev_half_tok[0]])
            m2 = em.op("V", lambda v: v.tensor_add(
                GHB.rearrange("p a b -> p (a b)"), SCR[:], TMA[:]),
                [ma, mb] + ([ymm_all[-1]] if ymm_all else []))
            prev_half_tok[0] = m2
            for b in range(4):
                pg_last_read[b] = m2
            if l0cut <= 3.5:
                continue
            for hh in range(8):
                h = half * 8 + hh
                j = h // 2
                rb = (h % 2) * 64
                pa_ap = PA[:, h // 8, (h % 8) * 64:(h % 8) * 64 + 64]
                w = [m2, evv[-1], pa_last_read[h // 8] if hh == 0 else None]
                ymm_all.append(em.op("T", lambda t, hh=hh, h=h, pa_ap=pa_ap: t.matmul(
                    pa_ap, GHB[:, hh, :], TB1[:, h * 64:h * 64 + 64], start=True, stop=False), w))
                ymm_all.append(em.op("T", lambda t, j=j, rb=rb, h=h, pa_ap=pa_ap: t.matmul(
                    pa_ap, RTB[rb:rb + 64, j, :], ST1B[rb:rb + 64, j, :], start=False, stop=True), [s0tok]))
        ylast = ymm_all[-1] if ymm_all else lt

        if l0cut <= 4:
            em.barrier()
            continue
        stm = []
        for h in range(16):
            j = h // 2
            rb = (h % 2) * 64
            w = [trkh[-1], evv[-1], pg_last_read[0] if h == 0 else None]
            stm.append(em.op("T", lambda t, j=j, rb=rb, h=h: t.matmul(
                PG[rb:rb + 64, 0, (h // 2) * 64:(h // 2) * 64 + 64],
                KHT[:, j, rb:rb + 64], TB1[:, h * 64:h * 64 + 64], start=True, stop=True), w))
        sup = []
        for pr in range(8):
            sup.append(em.op("V", lambda v, pr=pr: v.scalar_tensor_tensor(
                ST1[:, pr, :], ST1[:, pr, :], FMA[:, pr, 127:128], PG[:, 0, pr * 64:pr * 64 + 64],
                op0=ALU.mult, op1=ALU.add), [stm[-1], s0tok, ylast]))
        s0tok = em.op("S", lambda s: s.copy(ST1B.rearrange("p a b -> p (a b)"), ST1.rearrange("p a b -> p (a b)")), sup)
        pg_last_read[0] = s0tok

        if l0cut <= 5:
            em.barrier()
            continue
        gm = em.op("V", lambda v: v.tensor_reduce(STT[:, 16:32], PA.rearrange("p a b -> p (a b)").rearrange("p (h i) -> p h i", h=16), axis=AX, op=ALU.add), [ylast])
        gsq = em.op("S", lambda s: s.activation(SCR[:], PA.rearrange("p a b -> p (a b)"), AF.Square), [ylast, prev_half_tok[0]])
        gs = em.op("V", lambda v: v.tensor_reduce(STT[:, 0:16], SCR.rearrange("p (h i) -> p h i", h=16), axis=AX, op=ALU.add), [gsq])
        gmm = em.op("V", lambda v: v.tensor_scalar(STT[:, 16:32], STT[:, 16:32], 1.0 / DH, None, op0=ALU.mult), [gm])
        gv1 = em.op("V", lambda v: v.tensor_tensor(SCR[:, 0:16], STT[:, 16:32], STT[:, 16:32], op=ALU.mult), [gmm, gs])
        gv2 = em.op("V", lambda v: v.tensor_scalar(SCR[:, 16:32], STT[:, 0:16], 1.0 / DH, None, op0=ALU.mult), [gs])
        gv3 = em.op("V", lambda v: v.tensor_sub(SCR[:, 0:16], SCR[:, 16:32], SCR[:, 0:16]), [gv1, gv2])
        gsd = em.op("S", lambda s: s.activation(SCR[:, 16:32], SCR[:, 0:16], AF.Sqrt, bias=0.00064), [gv3])
        grs = em.op("V", lambda v: v.reciprocal(SCR[:, 0:16], SCR[:, 16:32]), [gsd])
        z1 = em.op("V", lambda v: v.tensor_tensor(TMA.rearrange("p (h i) -> p h i", h=16),
            PA.rearrange("p a b -> p (a b)").rearrange("p (h i) -> p h i", h=16),
            STT[:, 16:32, None].broadcast_to([128, 16, 64]), op=ALU.subtract), [gmm, ylast])
        z2 = em.op("V", lambda v: v.tensor_tensor(TMB.rearrange("p (h i) -> p h i", h=16),
            TMA.rearrange("p (h i) -> p h i", h=16),
            SCR[:, 0:16, None].broadcast_to([128, 16, 64]), op=ALU.mult), [z1, grs, stm[-1]])
        pa_last_read[0] = z2
        pa_last_read[1] = z2
        trz = transpose8(lambda j: TMB[:, j * 128:(j + 1) * 128], ZFB, [z2])
        if l0cut <= 6:
            em.barrier()
            continue
        oml = []
        for j in range(8):
            for half in range(2):
                slot = (j * 2 + half) % 4
                w = [trz[j], WSG[slot]]
                ld = em.dma("Q" if half == 0 else "G", WS[:, slot, :], wb["wo"][j * 128:(j + 1) * 128, half * 512:(half + 1) * 512], waits=w)
                wm = [ld] + [pa_last_read[half] if j == 0 else None]
                oml.append(em.op("T", lambda t, j=j, half=half, slot=slot: t.matmul(
                    PA[:, half, :], ZFB[:, j, :], WS[:, slot, :], start=(j == 0), stop=False), wm))
                WSG[slot] = oml[-1]
        for half in range(2):
            oml.append(em.op("T", lambda t, half=half: t.matmul(
                PA[:, half, :], ONEC[:], BIASR[0:1, half * 512:(half + 1) * 512], start=False, stop=True), [q0]))
        rt = []
        for half in range(2):
            rt.append(em.op("V", lambda v, half=half, i=i: v.tensor_add(
                X[:, i, half * 512:(half + 1) * 512], X[:, i, half * 512:(half + 1) * 512], PA[:, half, :]), [oml[-1]]))
        resid_tok[i] = rt[-1]
        pa_last_read[0] = rt[0]
        pa_last_read[1] = rt[1]
        em.barrier()

    em.barrier()
    for cm in reversed(ctx_l0):
        cm.__exit__(None, None, None)
    for cm in reversed(ctx_mid):
        cm.__exit__(None, None, None)

    if phase_stop <= 2:
        return finalize_early([])

    ctx_f = []
    def sbf(name, shape, dt=F32):
        return enter(nc.sbuf_tensor(name, list(shape), dt), ctx_f)
    XQF = sbf("XQF", [128, 2, 8, 128], BF16)
    HB = sbf("HB", [128, 2, 4096])
    FB = sbf("FB", [128, 4, 128], BF16)
    GSC = sbf("GSC", [128, 12])

    pa_g = [None, None]
    pg_g = [None, None, None, None]
    pt_g = None
    ffn_res = {}
    for sblk in range(NT // 2):
        tiles = [2 * sblk, 2 * sblk + 1]
        amax_toks = []
        gsc_toks = []
        for tt_, i in enumerate(tiles):
            lt = ln_tile(i, [resid_tok[i]])
            am = em.op("V", lambda v, tt_=tt_: v.reduce_max(STT[:, 8 + tt_:9 + tt_], XN[:], axis=AX, apply_absolute_value=True), [lt])
            am = em.op("V", lambda v, tt_=tt_: v.tensor_scalar(STT[:, 8 + tt_:9 + tt_], STT[:, 8 + tt_:9 + tt_], 1e-5, None, op0=ALU.max), [am])
            rc = em.op("V", lambda v, tt_=tt_: v.reciprocal(STT[:, 10 + tt_:11 + tt_], STT[:, 8 + tt_:9 + tt_]), [am])
            rc = em.op("V", lambda v, tt_=tt_: v.tensor_scalar(STT[:, 10 + tt_:11 + tt_], STT[:, 10 + tt_:11 + tt_], 127.0, None, op0=ALU.mult), [rc])
            q1 = em.op("V", lambda v, tt_=tt_: v.tensor_scalar(SCR[:], XN[:], STT[:, 10 + tt_:11 + tt_], MAGIC, op0=ALU.mult, op1=ALU.add), [rc])
            q2 = em.op("V", lambda v: v.tensor_scalar(SCR[:], SCR[:], MAGIC, 127.0, op0=ALU.subtract, op1=ALU.min), [q1])
            q3 = em.op("V", lambda v: v.tensor_scalar(TMB[:], SCR[:], -128.0, None, op0=ALU.max), [q2, amax_toks[-1][-1] if amax_toks else None])
            trq = transpose8(lambda j: TMB[:, j * 128:(j + 1) * 128], XQF[:, tt_], [q3])
            gsw1 = em.op("V", lambda v, tt_=tt_: v.tensor_scalar(GSC[:, tt_ * 6:tt_ * 6 + 1], STT[:, 8 + tt_:9 + tt_], MGB[:, 0:1], 1.0 / 127.0, op0=ALU.mult, op1=ALU.mult), [am, q0])
            gsw2 = em.op("V", lambda v, tt_=tt_: v.tensor_scalar(GSC[:, tt_ * 6 + 1:tt_ * 6 + 2], STT[:, 8 + tt_:9 + tt_], MGB[:, 1:2], 1.0 / 127.0, op0=ALU.mult, op1=ALU.mult), [am])
            gsw3 = em.op("V", lambda v, tt_=tt_: v.memset(GSC[:, tt_ * 6 + 2:tt_ * 6 + 3], 0.0), [])
            gsc_toks.append(gsw3)
            amax_toks.append(trq)
        hq_last = []
        tm_guard = {0: None, 1: None}
        for ns in range(8):
            mmg = []
            mmu = []
            for kj in range(8):
                sg_ = kj % 2
                su_ = 2 + kj % 2
                w = [amax_toks[0][-1], amax_toks[1][-1]]
                ldg = em.dma("Q", WS[:, sg_, :], wb["g"][kj * 128:(kj + 1) * 128, ns * 512:(ns + 1) * 512], waits=w + [WSG[sg_]])
                ldu = em.dma("G", WS[:, su_, :], wb["u"][kj * 128:(kj + 1) * 128, ns * 512:(ns + 1) * 512], waits=w + [WSG[su_]])
                for tt_ in range(2):
                    mmg.append(em.op("T", lambda t, kj=kj, tt_=tt_, sg_=sg_: t.matmul(
                        PA[:, tt_, :], XQF[:, tt_, kj, :], WS[:, sg_, :], start=(kj == 0), stop=(kj == 7)),
                        [ldg, pa_g[tt_] if kj == 0 else None]))
                    mmu.append(em.op("T", lambda t, kj=kj, tt_=tt_, su_=su_: t.matmul(
                        PG[:, tt_, :], XQF[:, tt_, kj, :], WS[:, su_, :], start=(kj == 0), stop=(kj == 7)),
                        [ldu, pg_g[tt_] if kj == 0 else None]))
                WSG[sg_] = mmg[-1]
                WSG[su_] = mmu[-1]
            hb_prev = None
            for tt_ in range(2):
                off = tt_ * 512
                sg1 = em.op("S", lambda s, tt_=tt_, off=off: s.activation(TMA[:, off:off + 512], PA[:, tt_, :], AF.Silu, scale=GSC[:, tt_ * 6:tt_ * 6 + 1]), [mmg[-1], gsc_toks[-1], tm_guard[tt_]])
                uv1 = em.op("V", lambda v, tt_=tt_, off=off: v.tensor_scalar(TMB[:, off:off + 512], PG[:, tt_, :], GSC[:, tt_ * 6 + 1:tt_ * 6 + 2], None, op0=ALU.mult), [mmu[-1], gsc_toks[-1], tm_guard[tt_]])
                hb1 = em.op("V", lambda v, tt_=tt_, ns=ns, off=off: v.tensor_mul(HB[:, tt_, ns * 512:(ns + 1) * 512], TMA[:, off:off + 512], TMB[:, off:off + 512]), [sg1, uv1])
                tm_guard[tt_] = hb1
                pa_g[tt_] = sg1
                pg_g[tt_] = uv1
                hm = em.op("V", lambda v, tt_=tt_, ns=ns: v.reduce_max(STT[:, 14 + tt_:15 + tt_], HB[:, tt_, ns * 512:(ns + 1) * 512], axis=AX, apply_absolute_value=True), [hb1])
                hq_last.append(em.op("V", lambda v, tt_=tt_: v.tensor_max(GSC[:, tt_ * 6 + 2:tt_ * 6 + 3], GSC[:, tt_ * 6 + 2:tt_ * 6 + 3], STT[:, 14 + tt_:15 + tt_]), [hm]))
        qdone = []
        for tt_ in range(2):
            am = em.op("V", lambda v, tt_=tt_: v.tensor_scalar(GSC[:, tt_ * 6 + 2:tt_ * 6 + 3], GSC[:, tt_ * 6 + 2:tt_ * 6 + 3], 1e-5, None, op0=ALU.max), hq_last)
            rc = em.op("V", lambda v, tt_=tt_: v.reciprocal(GSC[:, tt_ * 6 + 3:tt_ * 6 + 4], GSC[:, tt_ * 6 + 2:tt_ * 6 + 3]), [am])
            rc = em.op("V", lambda v, tt_=tt_: v.tensor_scalar(GSC[:, tt_ * 6 + 3:tt_ * 6 + 4], GSC[:, tt_ * 6 + 3:tt_ * 6 + 4], 127.0, None, op0=ALU.mult), [rc])
            em.op("V", lambda v, tt_=tt_: v.tensor_scalar(GSC[:, tt_ * 6 + 4:tt_ * 6 + 5], GSC[:, tt_ * 6 + 2:tt_ * 6 + 3], MGB[:, 2:3], 1.0 / 127.0, op0=ALU.mult, op1=ALU.mult), [am])
            for qs in range(4):
                q1 = em.op("V", lambda v, tt_=tt_, qs=qs: v.tensor_scalar(SCR[:], HB[:, tt_, qs * 1024:(qs + 1) * 1024], GSC[:, tt_ * 6 + 3:tt_ * 6 + 4], MAGIC, op0=ALU.mult, op1=ALU.add), [rc, qdone[-1] if qdone else None])
                q2 = em.op("V", lambda v: v.tensor_scalar(SCR[:], SCR[:], MAGIC, 127.0, op0=ALU.subtract, op1=ALU.min), [q1])
                qdone.append(em.op("V", lambda v, tt_=tt_, qs=qs: v.tensor_scalar(HB[:, tt_, qs * 1024:(qs + 1) * 1024], SCR[:], -128.0, None, op0=ALU.max), [q2]))
        fb_guard = [None] * 4
        dmm_by = {(tt_, half): [] for tt_ in range(2) for half in range(2)}
        for kj in range(32):
            par = kj % 2
            for half in range(2):
                wslot = half * 2 + par
                ld = em.dma("Q" if half == 0 else "G", WS[:, wslot, :], wb["d"][kj * 128:(kj + 1) * 128, half * 512:(half + 1) * 512],
                            waits=[WSG[wslot]])
                for tt_ in range(2):
                    fslot = tt_ * 2 + par
                    if half == 0:
                        tw = list(qdone) + [fb_guard[fslot], PT_DRAIN["PT"][0]]
                        tt1 = em.op("T", lambda t, kj=kj, tt_=tt_, fslot=fslot: t.transpose(
                            PT[:, fslot, :], HB[:, tt_, kj * 128:(kj + 1) * 128], IDT[:]), tw)
                        ev = em.op("S", lambda s, fslot=fslot: s.copy(FB[:, fslot, :], PT[:, fslot, :]), [tt1])
                        fb_guard[fslot] = ev
                        PT_DRAIN["PT"][0] = ev
                    pv = PA[:, half, :] if tt_ == 0 else PG[:, 2 + half, :]
                    gw = [fb_guard[fslot], ld]
                    if kj == 0:
                        gw.append(pa_g[half] if tt_ == 0 else pg_g[2 + half] if len(pg_g) > 2 + half else None)
                    mmtok = em.op("T", lambda t, pv=pv, fslot=fslot, wslot=wslot, kj=kj: t.matmul(
                        pv, FB[:, fslot, :], WS[:, wslot, :], start=(kj == 0), stop=(kj == 31)), gw)
                    dmm_by[(tt_, half)].append(mmtok)
                    fb_guard[fslot] = mmtok
                WSG[wslot] = dmm_by[(1, half)][-1]
        last_d = [dmm_by[(tt_, half)][-1] for tt_ in range(2) for half in range(2)]
        for tt_, i in enumerate(tiles):
            for half in range(2):
                pv = PA[:, half, :] if tt_ == 0 else PG[:, 2 + half, :]
                rt = em.op("V", lambda v, pv=pv, i=i, half=half, tt_=tt_: v.scalar_tensor_tensor(
                    X[:, i, half * 512:(half + 1) * 512], pv, GSC[:, tt_ * 6 + 4:tt_ * 6 + 5],
                    X[:, i, half * 512:(half + 1) * 512], op0=ALU.mult, op1=ALU.add), last_d)
                ffn_res[i] = rt
                if tt_ == 0:
                    pa_g[half] = rt
                else:
                    pg_g[2 + half] = rt
            pg_g[tt_] = ffn_res[i]
        em.barrier()

    em.barrier()
    for cm in reversed(ctx_f):
        cm.__exit__(None, None, None)

    if phase_stop <= 3:
        return finalize_early([])

    ctx_1 = []
    def sb1(name, shape, dt=F32):
        return enter(nc.sbuf_tensor(name, list(shape), dt), ctx_1)
    FWB = sb1("FWB", [128, 8, H], BF16)
    BWB = sb1("BWB", [128, 8, H], BF16)
    GW1B = sb1("GW1B", [128, 8, DH], BF16)
    GW2B = sb1("GW2B", [DH, D], BF16)
    DELTB = sb1("DELTB", [128, H])
    TRI = sb1("TRI", [128, 128])
    em.dma("Q", TRI[:], TRII[:])
    em.dma("G", FWB[:], FW.rearrange("(j p) h -> p j h", p=128))
    em.dma("G", BWB[:], BW.rearrange("(j p) h -> p j h", p=128))
    em.dma("G", GW1B[:], GW1.rearrange("(j p) l -> p j l", p=128))
    em.dma("G", GW2B[:], GW2[:])
    dlb = em.dma("Q", DELTB[:], DELTA[0:1, :].broadcast_to([128, H]))
    wl1 = ("Q", em.counts["Q"])
    wl1g = ("G", em.counts["G"])
    XNF = sb1("XNF", [128, 8, 128], BF16)
    K2B = sb1("K2B", [128, 8, 128], BF16)
    KTMB = sb1("KTMB", [128, D], BF16)
    K3TB = sb1("K3TB", [128, D], BF16)
    VTB = sb1("VTB", [128, D], BF16)
    PHB = sb1("PHB", [128, 8, 128], BF16)
    GHL = sb1("GHL", [128, 16, 128], BF16)
    HBD = sb1("HBD", [128, 8, 128], BF16)
    HVD = sb1("HVD", [128, 8, 128], BF16)
    CGP = sb1("CGP", [128, D])
    CGR = sb1("CGR", [128, D])
    CGX = sb1("CGX", [128, D])
    GTF = sb1("GTF", [128, D])
    EFT = sb1("EFT", [128, 16])
    RSC = sb1("RSC", [128, 80])
    SML = sb1("SML", [16, 160])
    SML2 = sb1("SML2", [16, 128])
    G1B = sb1("G1B", [64, 128], BF16)
    EFEF = sb1("EFEF", [128, 16])

    em.op("V", lambda v: v.memset(ST1.rearrange("p a b -> p (a b)"), 0.0), [])
    em.op("V", lambda v: v.memset(ST2.rearrange("p a b -> p (a b)"), 0.0), [])
    em.op("V", lambda v: v.memset(HBD.rearrange("p a b -> p (a b)"), 0.0), [])
    em.op("V", lambda v: v.memset(HVD.rearrange("p a b -> p (a b)"), 0.0), [])
    LAMBB = sb1("LAMBB", [128, D])
    LR8 = sb1("LR8", [8, 128])
    PAD1 = sb1("PAD1", [128, 128])
    PAD2 = sb1("PAD2", [128, 128])
    pad2z = em.op("V", lambda v: v.memset(PAD2[:], 0.0), [])
    g1c_guard = [None]
    _pz = em.op("V", lambda v: v.memset(PAD1[:], 0.0), [])
    _pc = em.op("V", lambda v: v.tensor_copy(PAD1[:, 0:8], LAMB[:]), [_pz])
    _e = transpose1("PT", PT[:, 0, :], LR8[:], PAD1[:], IDT[:], [_pc], ev_src=PT[0:8, 0, :])
    _d = em.dma("Q", LAMBD[:], LR8[:], waits=[_e])
    lrt = em.dma("Q", LAMBB[:], LAMBD[0:1, :].broadcast_to([128, D]), waits=[_d])

    pg_g = [None, None, None, None]
    pa_g = [None, None]
    stb_tok = ("S", em.counts["S"])
    l1_res = {}

    for i in range(NT):
        lt = ln_tile(i, [ffn_res[i]])
        trx = transpose8(lambda j: XN[:, j * 128:(j + 1) * 128], XNF, [lt])
        def tmproj2(wdr, out_fp32, out_bf, scale_guard=None):
            mms = []
            for j in range(8):
                for half in range(2):
                    slot = (j * 2 + half) % 4
                    w = [trx[j], WSG[slot]]
                    ld = em.dma("Q" if half == 0 else "G", WS[:, slot, :], wdr[j * 128:(j + 1) * 128, half * 512:(half + 1) * 512], waits=w)
                    mms.append(em.op("T", lambda t, j=j, half=half, slot=slot: t.matmul(
                        PA[:, half, :], XNF[:, j, :], WS[:, slot, :],
                        start=(j == 0), stop=(j == 7)), [ld, pa_g[half] if j == 0 else None]))
                    WSG[slot] = mms[-1]
            evs = []
            for half in range(2):
                if out_fp32 is not None:
                    evs.append(em.op("S", lambda s, half=half: s.copy(out_fp32[:, half * 512:(half + 1) * 512], PA[:, half, :]), [mms[-1]]))
                else:
                    evs.append(em.op("S", lambda s, half=half: s.copy(out_bf[:, half * 512:(half + 1) * 512], PA[:, half, :]), [mms[-1]]))
            pa_g[0] = evs[0]
            pa_g[1] = evs[1]
            return evs
        evq = tmproj2(wb["qw"], TMA, None)
        evk = tmproj2(wb["kw"], TMB, None)
        evv = tmproj2(wb["vw"], None, VTB)
        smm = []
        for j in range(8):
            w0 = [trx[j], wl1g] + ([pg_g[3]] if j == 0 else [])
            smm.append(em.op("T", lambda t, j=j: t.matmul(PG[:, 3, 0:16], XNF[:, j, :], FWB[:, j, :], start=(j == 0), stop=(j == 7)), w0))
        for j in range(8):
            smm.append(em.op("T", lambda t, j=j: t.matmul(PG[:, 3, 16:32], XNF[:, j, :], BWB[:, j, :], start=(j == 0), stop=(j == 7)), [trx[j]]))
        for j in range(8):
            smm.append(em.op("T", lambda t, j=j: t.matmul(PG[:, 3, 32:96], XNF[:, j, :], GW1B[:, j, :], start=(j == 0), stop=(j == 7)), [trx[j]]))
        sml = smm[-1]
        zfd = em.op("V", lambda v: v.tensor_add(RSC[:, 16:32], PG[:, 3, 0:16], DELTB[:]), [sml, dlb])
        sp0 = em.op("S", lambda s: s.activation(RSC[:, 32:48], RSC[:, 16:32], AF.Exp, scale=-1.0), [zfd])
        spl = em.op("S", lambda s: s.activation(PAD1[:, 0:16], RSC[:, 32:48], AF.Ln, bias=1.0), [sp0, _e])
        bet = em.op("S", lambda s: s.activation(RSC[:, 48:64], PG[:, 3, 16:32], AF.Sigmoid), [sml])
        g1c = em.op("S", lambda s: s.copy(PAD2[:, 0:64], PG[:, 3, 32:96]), [sml, g1c_guard[0]])
        pg_g[3] = g1c
        g1f = transpose1("PT", PT[:, 1, :], G1B[:], PAD2[:], IDT[:], [g1c, pad2z], ev_src=PT[0:64, 1, :])
        g1c_guard[0] = g1f
        gmm = []
        for half in range(2):
            gmm.append(em.op("T", lambda t, half=half: t.matmul(
                PA[:, half, :], G1B[:], GW2B[:, half * 512:(half + 1) * 512], start=True, stop=True),
                [g1f, wl1g, pa_g[half]]))
        gsg = []
        for half in range(2):
            gsg.append(em.op("S", lambda s, half=half: s.activation(GTF[:, half * 512:(half + 1) * 512], PA[:, half, :], AF.Sigmoid), [gmm[half]]))
        pa_g[0] = gsg[0]
        pa_g[1] = gsg[1]
        ev1 = transpose1("PT", PT[:, 1, :], SML[:, 0:128], PAD1[:], IDT[:], [spl], ev_src=PT[0:16, 1, :])
        scn = em.op("V", lambda v: v.tensor_tensor_scan(SML[:, 0:128], SML[:, 0:128], SML[:, 0:128], 0.0, op0=ALU.add, op1=ALU.bypass), [ev1])
        ev2 = transpose1("PT", PT[:, 2, 0:16], RSC[:, 64:80], SML[:, 0:128], IDT[0:16, 0:16], [scn])
        eft = em.op("S", lambda s: s.activation(EFT[:], RSC[:, 64:80], AF.Exp, scale=-1.0), [ev2])
        enf = em.op("S", lambda s: s.activation(RSC[:, 16:32], RSC[:, 64:80], AF.Exp), [ev2, sp0])
        cft = em.op("V", lambda v: v.tensor_mul(RSC[:, 16:32], RSC[:, 16:32], RSC[:, 48:64]), [enf, bet])
        ksq = em.op("S", lambda s: s.activation(SCR[:], TMB[:], AF.Square), evk)
        kss = em.op("V", lambda v: v.tensor_reduce(RSC[:, 0:16], SCR.rearrange("p (h i) -> p h i", h=H), axis=AX, op=ALU.add), [ksq])
        ksd = em.op("S", lambda s: s.activation(RSC[:, 0:16], RSC[:, 0:16], AF.Sqrt, bias=1e-6), [kss])
        krc = em.op("V", lambda v: v.reciprocal(RSC[:, 0:16], RSC[:, 0:16]), [ksd])
        knm = em.op("V", lambda v: v.tensor_tensor(TMB.rearrange("p (h i) -> p h i", h=H), TMB.rearrange("p (h i) -> p h i", h=H), RSC[:, 0:16, None].broadcast_to([128, H, DH]), op=ALU.mult), [krc])
        qsq = em.op("S", lambda s: s.activation(SCR[:], TMA[:], AF.Square), evq + [knm])
        qss = em.op("V", lambda v: v.tensor_reduce(RSC[:, 0:16], SCR.rearrange("p (h i) -> p h i", h=H), axis=AX, op=ALU.add), [qsq])
        qsd = em.op("S", lambda s: s.activation(RSC[:, 0:16], RSC[:, 0:16], AF.Sqrt, bias=1e-6), [qss])
        qrc = em.op("V", lambda v: v.reciprocal(RSC[:, 0:16], RSC[:, 0:16]), [qsd])
        qnm = em.op("V", lambda v: v.tensor_tensor(CGP.rearrange("p (h i) -> p h i", h=H), TMA.rearrange("p (h i) -> p h i", h=H), RSC[:, 0:16, None].broadcast_to([128, H, DH]), op=ALU.mult), [qrc])
        nend = em.op("V", lambda v: v.tensor_scalar(SML[:, 128:129], SML[:, 127:128], -1.0, None, op0=ALU.mult), [scn])
        e3r = em.op("S", lambda s: s.activation(SML2[:, 0:128], SML[:, 0:128], AF.Exp, scale=1.0, bias=SML[:, 128:129]), [nend])
        pool_pt = PT_DRAIN["PT"]
        tre = em.op("T", lambda t: t.transpose(PT[:, 0, 0:16], SML2[:, 0:128], IDT[0:16, 0:16]), [e3r, pool_pt[0]])
        ev3 = em.op("S", lambda s: s.copy(SCR[:, 0:16], PT[0:128, 0, 0:16]), [tre, qss])
        pool_pt[0] = ev3
        c3t = em.op("V", lambda v: v.tensor_mul(SCR[:, 0:16], SCR[:, 0:16], RSC[:, 48:64]), [ev3, bet])
        k3b = em.op("V", lambda v: v.tensor_tensor(K3TB.rearrange("p (h i) -> p h i", h=H), TMB.rearrange("p (h i) -> p h i", h=H), SCR[:, 0:16, None].broadcast_to([128, H, DH]), op=ALU.mult), [c3t, knm])
        ktc = em.op("S", lambda s: s.copy(KTMB[:], TMB[:]), [knm])
        k2tm = em.op("V", lambda v: v.tensor_tensor(TMA.rearrange("p (h i) -> p h i", h=H), TMB.rearrange("p (h i) -> p h i", h=H), RSC[:, 16:32, None].broadcast_to([128, H, DH]), op=ALU.mult), [cft, knm, qnm])
        trk2 = transpose8(lambda j: TMA[:, j * 128:(j + 1) * 128], K2B, [k2tm])
        efr_d = em.dma("Q", EFRD[:], EFT[127:128, :], waits=[eft])
        ebc = em.dma("Q", EFEF[:], EFRD[0:1, :].broadcast_to([128, H]), waits=[efr_d])
        hb_toks = []
        for pr in range(8):
            hb_toks.append(em.op("S", lambda s, pr=pr: s.copy(HBD[0:64, pr, 0:64], ST1[0:64, pr, :]), [stb_tok]))
            hb_toks.append(em.op("S", lambda s, pr=pr: s.copy(HBD[64:128, pr, 64:128], ST1[64:128, pr, :]), [stb_tok]))
            hb_toks.append(em.op("S", lambda s, pr=pr: s.copy(HVD[0:64, pr, 0:64], ST2[0:64, pr, :]), [stb_tok]))
            hb_toks.append(em.op("S", lambda s, pr=pr: s.copy(HVD[64:128, pr, 64:128], ST2[64:128, pr, :]), [stb_tok]))
        hbt = hb_toks[-1]
        stm = []
        for h in range(16):
            rb = (h % 2) * 64
            cc = (h // 2) * 64
            w = [k3b, ktc, evv[-1]] + ([pg_g[1]] if h == 0 else [])
            stm.append(em.op("T", lambda t, h=h, rb=rb, cc=cc: t.matmul(
                PG[rb:rb + 64, 1, cc:cc + 64], K3TB[:, h * 64:h * 64 + 64], KTMB[:, h * 64:h * 64 + 64],
                start=True, stop=True), w))
            w2 = [k3b, evv[-1]] + ([pg_g[2]] if h == 0 else [])
            stm.append(em.op("T", lambda t, h=h, rb=rb, cc=cc: t.matmul(
                PG[rb:rb + 64, 2, cc:cc + 64], K3TB[:, h * 64:h * 64 + 64], VTB[:, h * 64:h * 64 + 64],
                start=True, stop=True), w2))
        sup = []
        for pr in range(8):
            for hf in range(2):
                rs_ = slice(64 * hf, 64 * hf + 64)
                h_ = 2 * pr + hf
                sup.append(em.op("V", lambda v, pr=pr, rs_=rs_, h_=h_: v.scalar_tensor_tensor(
                    ST1[rs_, pr, :], ST1[rs_, pr, :], EFEF[rs_, h_:h_ + 1], PG[rs_, 1, pr * 64:pr * 64 + 64],
                    op0=ALU.mult, op1=ALU.add), [stm[-1], ebc, hbt]))
                sup.append(em.op("V", lambda v, pr=pr, rs_=rs_, h_=h_: v.scalar_tensor_tensor(
                    ST2[rs_, pr, :], ST2[rs_, pr, :], EFEF[rs_, h_:h_ + 1], PG[rs_, 2, pr * 64:pr * 64 + 64],
                    op0=ALU.mult, op1=ALU.add), [stm[-1], ebc, hbt]))
        stb_tok = sup[-1]
        pg_g[1] = sup[-1]
        pg_g[2] = sup[-1]

        cgi1 = em.op("S", lambda s: s.copy(CGR[:], CGP[:]), [qnm])
        cgi2 = em.op("V", lambda v: v.memset(CGX[:], 0.0), [qnm])
        isq = em.op("S", lambda s: s.activation(SCR[:], CGP[:], AF.Square), [qnm, k3b])
        irs = em.op("V", lambda v: v.tensor_reduce(RSC[:, 0:16], SCR.rearrange("p (h i) -> p h i", h=H), axis=AX, op=ALU.add), [isq])

        ghl_tok = None
        lastT = None
        pprev = None
        for it in range(CG_ITERS + 1):
            final = (it == CG_ITERS)
            src = CGX if final else CGP
            dep = [eft, irs, pprev] if it == 0 else [eft, pprev]
            ph = em.op("V", lambda v, src=src: v.tensor_tensor(SCR.rearrange("p (h i) -> p h i", h=H), src.rearrange("p (h i) -> p h i", h=H), EFT[:, :, None].broadcast_to([128, H, DH]), op=ALU.mult), dep + ([cgi2] if final else []))
            pool = PT_DRAIN["PT"]
            trp = []
            gmms = []
            for j in range(8):
                tt = em.op("T", lambda t, j=j: t.transpose(PT[:, j % 4, :], SCR[:, j * 128:(j + 1) * 128], IDT[:]), [ph, pool[0]])
                ev = em.op("S", lambda s, j=j: s.copy(PHB[:, j, :], PT[:, j % 4, :]), [tt])
                pool[0] = ev
                trp.append(ev)
                for h in (2 * j, 2 * j + 1):
                    rb = (h % 2) * 64
                    w = [trk2[j], ev, gmms[-1] if gmms else None] + ([ghl_tok] if h % 4 == 0 else [])
                    gmms.append(em.op("T", lambda t, h=h, j=j, rb=rb: t.matmul(
                        PG[:, h // 4, (h % 4) * 128:(h % 4) * 128 + 128],
                        K2B[rb:rb + 64, j, :], PHB[rb:rb + 64, j, :], start=True, stop=True), w))
            mev = []
            for bb in range(2):
                mev.append(em.op("V", lambda v, bb=bb: v.tensor_tensor(
                    GHL[:, bb * 8:(bb + 1) * 8, :],
                    PG[:, bb * 2:(bb + 1) * 2, :].rearrange("p a b -> p (a b)").rearrange("p (a b) -> p a b", a=8),
                    TRI[:, None, :].broadcast_to([128, 8, 128]), op=ALU.mult), gmms + [wl1]))
            ghl_tok = mev[-1]
            amms = []
            for h in range(16):
                w = [mev[h // 8], ktc if not final else None] + ([pa_g[h // 8]] if h % 8 == 0 else [])
                amms.append(em.op("T", lambda t, h=h, final=final: t.matmul(
                    PA[:, h // 8, (h % 8) * 64:(h % 8) * 64 + 64], GHL[:, h, :],
                    (VTB if final else KTMB)[:, h * 64:h * 64 + 64], start=(h % 8 == 0), stop=False), w))
            for j in range(8):
                amms.append(em.op("T", lambda t, j=j, final=final: t.matmul(
                    PA[:, j // 4, (j % 4) * 128:(j % 4) * 128 + 128], PHB[:, j, :],
                    (HVD if final else HBD)[:, j, :], start=False, stop=(j % 4 == 3)), [trp[j], hbt]))
            lastT = amms[-1]
            if final:
                break
            lp = em.op("V", lambda v: v.tensor_tensor(SCR[:], CGP[:], LAMBB[:], op=ALU.mult), [trp[-1], lrt])
            ap1 = em.op("V", lambda v: v.tensor_add(TMA[:, 0:512], PA[:, 0, :], SCR[:, 0:512]), [lp, amms[-2], amms[-1]])
            ap2 = em.op("V", lambda v: v.tensor_add(TMA[:, 512:1024], PA[:, 1, :], SCR[:, 512:1024]), [lp, amms[-2], amms[-1]])
            pa_g[0] = ap1
            pa_g[1] = ap2
            pp1 = em.op("V", lambda v: v.tensor_mul(SCR[:], CGP[:], TMA[:]), [ap1, ap2])
            pp2 = em.op("V", lambda v: v.tensor_reduce(RSC[:, 16:32], SCR.rearrange("p (h i) -> p h i", h=H), axis=AX, op=ALU.add), [pp1])
            al1 = em.op("V", lambda v: v.tensor_scalar(RSC[:, 32:48], RSC[:, 16:32], 1e-12, None, op0=ALU.add), [pp2])
            al2 = em.op("V", lambda v: v.reciprocal(RSC[:, 32:48], RSC[:, 32:48]), [al1])
            al3 = em.op("V", lambda v: v.tensor_mul(RSC[:, 32:48], RSC[:, 32:48], RSC[:, 0:16]), [al2])
            x1 = em.op("V", lambda v: v.tensor_tensor(SCR.rearrange("p (h i) -> p h i", h=H), CGP.rearrange("p (h i) -> p h i", h=H), RSC[:, 32:48, None].broadcast_to([128, H, DH]), op=ALU.mult), [al3, pp1])
            x2 = em.op("V", lambda v: v.tensor_add(CGX[:], CGX[:], SCR[:]), [x1])
            r1 = em.op("V", lambda v: v.tensor_tensor(SCR.rearrange("p (h i) -> p h i", h=H), TMA.rearrange("p (h i) -> p h i", h=H), RSC[:, 32:48, None].broadcast_to([128, H, DH]), op=ALU.mult), [x2])
            r2 = em.op("V", lambda v: v.tensor_sub(CGR[:], CGR[:], SCR[:]), [r1, cgi1])
            rq1 = em.op("S", lambda s: s.activation(SCR[:], CGR[:], AF.Square), [r2])
            rq2 = em.op("V", lambda v: v.tensor_reduce(RSC[:, 48:64], SCR.rearrange("p (h i) -> p h i", h=H), axis=AX, op=ALU.add), [rq1])
            b1 = em.op("V", lambda v: v.tensor_scalar(RSC[:, 64:80], RSC[:, 0:16], 1e-12, None, op0=ALU.add), [rq2])
            b2 = em.op("V", lambda v: v.reciprocal(RSC[:, 64:80], RSC[:, 64:80]), [b1])
            b3 = em.op("V", lambda v: v.tensor_mul(RSC[:, 64:80], RSC[:, 64:80], RSC[:, 48:64]), [b2])
            rcp = em.op("V", lambda v: v.tensor_copy(RSC[:, 0:16], RSC[:, 48:64]), [b3])
            p1 = em.op("V", lambda v: v.tensor_tensor(SCR.rearrange("p (h i) -> p h i", h=H), CGP.rearrange("p (h i) -> p h i", h=H), RSC[:, 64:80, None].broadcast_to([128, H, DH]), op=ALU.mult), [b3, rq1])
            p2 = em.op("V", lambda v: v.tensor_add(CGP[:], CGR[:], SCR[:]), [p1])
            irs = rcp
            pprev = p2

        og1 = em.op("V", lambda v: v.tensor_mul(TMA[:, 0:512], PA[:, 0, :], GTF[:, 0:512]), [lastT] + gsg)
        og2 = em.op("V", lambda v: v.tensor_mul(TMA[:, 512:1024], PA[:, 1, :], GTF[:, 512:1024]), [lastT])
        pa_g[0] = og1
        pa_g[1] = og2
        osq = em.op("S", lambda s: s.activation(SCR[:], TMA[:], AF.Square), [og1, og2])
        oss = em.op("V", lambda v: v.tensor_reduce(RSC[:, 0:16], SCR.rearrange("p (h i) -> p h i", h=H), axis=AX, op=ALU.add), [osq])
        om1 = em.op("V", lambda v: v.tensor_scalar(RSC[:, 0:16], RSC[:, 0:16], 1.0 / DH, None, op0=ALU.mult), [oss])
        om2 = em.op("S", lambda s: s.activation(RSC[:, 16:32], RSC[:, 0:16], AF.Sqrt, bias=1e-5), [om1])
        om3 = em.op("V", lambda v: v.reciprocal(RSC[:, 16:32], RSC[:, 16:32]), [om2])
        zb = em.op("V", lambda v: v.tensor_tensor(TMB.rearrange("p (h i) -> p h i", h=H), TMA.rearrange("p (h i) -> p h i", h=H), RSC[:, 16:32, None].broadcast_to([128, H, DH]), op=ALU.mult), [om3])
        trz = transpose8(lambda j: TMB[:, j * 128:(j + 1) * 128], XNF, [zb])
        oml = []
        for j in range(8):
            for half in range(2):
                slot = (j * 2 + half) % 4
                w = [trz[j], WSG[slot]]
                ld = em.dma("Q" if half == 0 else "G", WS[:, slot, :], wb["ow"][j * 128:(j + 1) * 128, half * 512:(half + 1) * 512], waits=w)
                oml.append(em.op("T", lambda t, j=j, half=half, slot=slot: t.matmul(
                    PA[:, half, :], XNF[:, j, :], WS[:, slot, :], start=(j == 0), stop=(j == 7)),
                    [ld, pa_g[half] if j == 0 else None]))
                WSG[slot] = oml[-1]
        rt = []
        for half in range(2):
            rt.append(em.op("V", lambda v, half=half, i=i: v.tensor_add(
                X[:, i, half * 512:(half + 1) * 512], X[:, i, half * 512:(half + 1) * 512], PA[:, half, :]), [oml[-1]]))
        l1_res[i] = rt[-1]
        pa_g[0] = rt[0]
        pa_g[1] = rt[1]
        em.barrier()

    em.barrier()
    for cm in reversed(ctx_1):
        cm.__exit__(None, None, None)

    if phase_stop <= 4:
        return finalize_early([])

    ctx_2 = []
    XQ2 = enter(nc.sbuf_tensor("XQ2", [128, 8, 128], BF16), ctx_2)
    RSB = enter(nc.sbuf_tensor("RSB", [128, D], F32), ctx_2)
    rsl = em.dma("Q", RSB[:], RSRD[0:1, :].broadcast_to([128, D]))
    pa_g = [None, None]
    store_toks = []
    for i in range(NT):
        lt = ln_tile(i, [l1_res[i]])
        am = em.op("V", lambda v: v.reduce_max(STT[:, 8:9], XN[:], axis=AX, apply_absolute_value=True), [lt])
        am = em.op("V", lambda v: v.tensor_scalar(STT[:, 8:9], STT[:, 8:9], 1e-12, None, op0=ALU.max), [am])
        ls = em.op("V", lambda v: v.tensor_scalar(STT[:, 9:10], STT[:, 8:9], 1.0 / 127.0, None, op0=ALU.mult), [am])
        rc = em.op("V", lambda v: v.reciprocal(STT[:, 10:11], STT[:, 8:9]), [am])
        rc = em.op("V", lambda v: v.tensor_scalar(STT[:, 10:11], STT[:, 10:11], 127.0, None, op0=ALU.mult), [rc])
        q1 = em.op("V", lambda v: v.tensor_scalar(SCR[:], XN[:], STT[:, 10:11], MAGIC, op0=ALU.mult, op1=ALU.add), [rc])
        q2 = em.op("V", lambda v: v.tensor_scalar(SCR[:], SCR[:], MAGIC, 127.0, op0=ALU.subtract, op1=ALU.min), [q1])
        q3 = em.op("V", lambda v: v.tensor_scalar(TMB[:], SCR[:], -127.0, None, op0=ALU.max), [q2])
        trq = transpose8(lambda j: TMB[:, j * 128:(j + 1) * 128], XQ2, [q3])
        mms = []
        for j in range(8):
            for half in range(2):
                slot = (j * 2 + half) % 4
                w = [trq[j], WSG[slot]]
                ld = em.dma("Q" if half == 0 else "G", WS[:, slot, :], wb["rhs"][j * 128:(j + 1) * 128, half * 512:(half + 1) * 512], waits=w)
                mms.append(em.op("T", lambda t, j=j, half=half, slot=slot: t.matmul(
                    PA[:, half, :], XQ2[:, j, :], WS[:, slot, :], start=(j == 0), stop=(j == 7)),
                    [ld, pa_g[half] if j == 0 else None]))
                WSG[slot] = mms[-1]
        for half in range(2):
            t1 = em.op("V", lambda v, half=half: v.tensor_tensor(SCR[:, half * 512:(half + 1) * 512], PA[:, half, :], RSB[:, half * 512:(half + 1) * 512], op=ALU.mult), [mms[-1], rsl if i == 0 else None])
            t2 = em.op("V", lambda v, half=half, i=i: v.scalar_tensor_tensor(
                X[:, i, half * 512:(half + 1) * 512], SCR[:, half * 512:(half + 1) * 512], STT[:, 9:10],
                X[:, i, half * 512:(half + 1) * 512], op0=ALU.mult, op1=ALU.add), [t1, ls])
            pa_g[half] = t2
        st = em.dma("Q", OUT[:, i, :], X[:, i, :], waits=[pa_g[0], pa_g[1]])
        store_toks.append(st)
        em.barrier()

    em.op("V", lambda v: v.engine_nop() if False else v.memset(STT[:, 31:32], 0.0), [store_toks[-1]])
    for cm in reversed(ctx_2):
        cm.__exit__(None, None, None)

    with nc.Block() as block:
        em.install(block, sems)
    for cm in reversed(ctx_global):
        cm.__exit__(None, None, None)
    return nc


def host_layout(inputs):
    g = {}
    x = np.ascontiguousarray(inputs["L0_attn_query"], dtype=np.float32)
    B = x.shape[0]
    xs = [np.ascontiguousarray(x[b].reshape(NT, 128, D).transpose(1, 0, 2)) for b in range(B)]

    def fmcols(vec):
        return np.ascontiguousarray(np.asarray(vec, np.float32).reshape(8, 128).T)

    g["maax"] = fmcols(inputs["L0_attn_time_maa_x"])
    maacol = np.stack([fmcols(inputs["L0_attn_time_maa_w"]), fmcols(inputs["L0_attn_time_maa_k"]),
                       fmcols(inputs["L0_attn_time_maa_v"]), fmcols(inputs["L0_attn_time_maa_r"])], axis=1)
    g["maacol"] = np.ascontiguousarray(maacol)
    g["maaw1"] = np.ascontiguousarray(inputs["L0_attn_time_maa_w1"], np.float32)
    w2 = np.asarray(inputs["L0_attn_time_maa_w2"], np.float32)
    w2l = np.zeros((64, 16, 128), np.float32)
    for f in range(4):
        for j in range(8):
            w2l[32 * (f % 2):32 * (f % 2) + 32, (f // 2) * 8 + j, :] = w2[f, :, j * 128:(j + 1) * 128]
    g["w2l"] = np.ascontiguousarray(w2l)
    g["dw1"] = np.ascontiguousarray(inputs["L0_attn_time_decay_w1"], np.float32)
    dw2l = np.zeros((65, 8, 128), np.float32)
    td2 = np.asarray(inputs["L0_attn_time_decay_w2"], np.float32)
    for j in range(8):
        dw2l[0:64, j, :] = td2[:, j * 128:(j + 1) * 128]
        dw2l[64, j, :] = np.asarray(inputs["L0_attn_time_decay"], np.float32)[j * 128:(j + 1) * 128]
    g["dw2l"] = np.ascontiguousarray(dw2l)
    g["ufm"] = fmcols(np.asarray(inputs["L0_attn_time_faaaa"], np.float32).reshape(D))
    g["lnw"] = fmcols(inputs["L0_attn_ln_w"])
    g["lnb"] = fmcols(inputs["L0_attn_ln_b"])
    g["wr"] = np.ascontiguousarray(inputs["L0_attn_W_r"], np.float32)
    g["wk"] = np.ascontiguousarray(inputs["L0_attn_W_k"], np.float32)
    g["wv"] = np.ascontiguousarray(inputs["L0_attn_W_v"], np.float32)
    g["wo"] = np.ascontiguousarray(inputs["L0_attn_W_o"], np.float32)
    g["wgt"] = np.ascontiguousarray(np.asarray(inputs["L0_ffn_w_gate"], np.float32).T)
    g["wut"] = np.ascontiguousarray(np.asarray(inputs["L0_ffn_w_up"], np.float32).T)
    g["wdt"] = np.ascontiguousarray(np.asarray(inputs["L0_ffn_w_down"], np.float32).T)
    g["qw"] = np.ascontiguousarray(inputs["L1_attn_Wq"], np.float32)
    g["kw"] = np.ascontiguousarray(inputs["L1_attn_Wk"], np.float32)
    g["vw"] = np.ascontiguousarray(inputs["L1_attn_Wv"], np.float32)
    g["fw"] = np.ascontiguousarray(inputs["L1_attn_Wf"], np.float32)
    g["bw"] = np.ascontiguousarray(inputs["L1_attn_Wbet"], np.float32)
    g["ow"] = np.ascontiguousarray(inputs["L1_attn_Wo"], np.float32)
    g["delta"] = np.ascontiguousarray(np.asarray(inputs["L1_attn_delta"], np.float32).reshape(1, H))
    g["lambf"] = fmcols(inputs["L1_attn_lamb_params"])
    g["nrmw"] = fmcols(inputs["L1_attn_norm_weight"])
    g["gw1"] = np.ascontiguousarray(inputs["L1_attn_gate_w1"], np.float32)
    g["gw2"] = np.ascontiguousarray(inputs["L1_attn_gate_w2"], np.float32)
    g["rhs"] = np.ascontiguousarray(inputs["L1_ffn_rhs"], np.float32)
    g["rhst"] = np.ascontiguousarray(np.asarray(inputs["L1_ffn_rhs"], np.float32).T)
    g["idf"] = np.eye(128, dtype=np.float32)
    g["idu8"] = np.ascontiguousarray(np.eye(128, dtype=np.uint8))
    cv = np.zeros(8, np.float32)
    cv[:len(CONST_VALS)] = CONST_VALS
    g["consti"] = np.ascontiguousarray(np.tile(cv, (128, 1)))
    st = np.arange(128)
    g["trii"] = np.ascontiguousarray((st[:, None] <= st[None, :]).astype(np.float32))
    g["tris"] = np.ascontiguousarray((st[:, None] < st[None, :]).astype(np.float32))
    g["eyef"] = np.eye(128, dtype=np.float32)
    return g, xs


_NC_CACHE = {}


def kernel(**inputs):
    if "nc" not in _NC_CACHE:
        _NC_CACHE["nc"] = build_program()
    nc = _NC_CACHE["nc"]
    g, xs = host_layout(inputs)
    in_maps = []
    for b in range(8):
        m = dict(g)
        m["x0"] = xs[b]
        in_maps.append(m)
    res = run_bass_kernel_spmd(nc, in_maps, core_ids=list(range(8)))
    out = np.zeros((8, 4096, D), np.float32)
    for b in range(8):
        ob = res.results[b]["out"]
        out[b] = ob.transpose(1, 0, 2).reshape(4096, D)
    return out
